# version 16
# baseline (speedup 1.0000x reference)
import contextlib
import numpy as np
import ml_dtypes
import concourse.bass as bass
import concourse.mybir as mybir
from concourse.bass_utils import run_bass_kernel_spmd

F32 = mybir.dt.float32
BF16 = mybir.dt.bfloat16
AF = mybir.ActivationFunctionType
ALU = mybir.AluOpType
BF = ml_dtypes.bfloat16

DEBUG = False
STOP = None

DM, DI, G, N, H, P = 4096, 8192, 8, 128, 128, 64
CONVD = DI + 2 * G * N
PROJD = DI + CONVD + H
TOK, THL = 1024, 1027
NCD = DM // 128
NCH = 8
EPS = 1e-5
TT0 = [3, 515, 0]
TTW = [512, 512, 3]
OC_DT = 144
OC_ORDER = list(range(128, 144)) + list(range(64, 128)) + list(range(0, 64))


class TK:

    def __init__(self, nc):
        self.nc = nc
        self.sems = {}
        self.cnt = {}
        self.tokens = {}
        self.resv = {}

    def inc(self, instr, stream, name, n=1):
        if stream not in self.sems:
            self.sems[stream] = self.nc.alloc_semaphore("tk_" + stream)
            self.cnt[stream] = 0
        instr.then_inc(self.sems[stream], n)
        self.cnt[stream] += n
        self.tokens[name] = (self.sems[stream], self.cnt[stream])
        return instr

    def reserve(self, stream, name, n=1):
        if stream not in self.sems:
            self.sems[stream] = self.nc.alloc_semaphore("tk_" + stream)
            self.cnt[stream] = 0
        self.cnt[stream] += n
        self.tokens[name] = (self.sems[stream], self.cnt[stream])
        self.resv.setdefault(stream, []).append((name, n))

    def bind(self, instr, name):
        for stream, lst in self.resv.items():
            if lst and lst[0][0] == name:
                nm, n = lst.pop(0)
                instr.then_inc(self.sems[stream], n)
                return instr
        raise KeyError(f"bind out of order or unreserved: {name}")

    def tok(self, name):
        return self.tokens[name]

    def wait(self, eng, *names):
        for nm in names:
            s, c = self.tokens[nm]
            eng.wait_ge(s, c)


def build_nc():
    nc = bass.Bass("TRN2", target_bir_lowering=False, detect_race_conditions=False)
    tk = TK(nc)

    DT = nc.dram_tensor
    xfm = DT("xfm", [DM, THL], F32, kind="ExternalInput").ap()
    WinT = DT("WinT", [PROJD // 128, 128, DM], F32, kind="ExternalInput").ap()
    WoutT = DT("WoutT", [32, 128, DI], F32, kind="ExternalInput").ap()
    Wffn = DT("Wffn", [32, 128, DM], F32, kind="ExternalInput").ap()
    nw_in = DT("nw_re", [128, NCD], F32, kind="ExternalInput").ap()
    cw_in = DT("cw_re", [128, 320], F32, kind="ExternalInput").ap()
    cb_in = DT("cb_re", [128, 80], F32, kind="ExternalInput").ap()
    dtb_in = DT("dtb_re", [128, 1], F32, kind="ExternalInput").ap()
    alog_in = DT("alog_re", [128, 1], F32, kind="ExternalInput").ap()
    dp_in = DT("dp_re", [1, 128], F32, kind="ExternalInput").ap()
    gnw_in = DT("gnw_re", [128, 64], F32, kind="ExternalInput").ap()
    dpoc_in = DT("dpoc_re", [128, 64], F32, kind="ExternalInput").ap()
    g16_in = DT("g16_re", [1, 16], F32, kind="ExternalInput").ap()
    selt_in = DT("selt", [8, 8], F32, kind="ExternalInput").ap()
    maskv_in = DT("maskv", [8, 1], F32, kind="ExternalInput").ap()
    lA_in = DT("lA_re", [16, 128, 1024], F32, kind="ExternalInput").ap()
    lB_in = DT("lB_re", [16, 128, 1024], F32, kind="ExternalInput").ap()
    lAs_in = DT("lAs_re", [128, NCD * 32], F32, kind="ExternalInput").ap()
    lBs_in = DT("lBs_re", [128, 1024], F32, kind="ExternalInput").ap()
    ffnb_in = DT("ffnb_re", [128, NCD], F32, kind="ExternalInput").ap()
    out_fm = DT("out_fm", [DM, TOK], F32, kind="ExternalOutput").ap()

    hidf_d = DT("hidf_d", [DI, TOK], BF16).ap()
    hidt_d = DT("hidt_d", [NCH, 128, DI], BF16).ap()
    gate_d = DT("gate_d", [DI, TOK], BF16).ap()
    C_d = DT("C_dd", [G * N, TOK], BF16).ap()
    s_d = DT("s_dd", [NCH, N, H * P], BF16).ap()
    g_d = DT("g_dd", [DM, TOK], BF16).ap()
    y_d = DT("y_dd", [DI, TOK], BF16).ap()
    ccSi = DT("ccSi", [N, H * P], BF16).ap()
    ccSo = DT("ccSo", [8 * N, H * P], BF16).ap()
    ccLi = DT("ccLi", [H, 1], F32).ap()
    ccLo = DT("ccLo", [8 * H, 1], F32).ap()

    dbg = {}
    if DEBUG:
        for nm, shp in [("d_xcn", [DM, THL]), ("d_dt", [128, THL]),
                        ("d_sloc", [N, H * P]), ("d_sin", [N, H * P]),
                        ("d_xn", [DM, TOK]), ("d_mu", [128, THL]),
                        ("d_ct", [128, THL]), ("d_y2", [DI, TOK])]:
            dbg[nm] = DT(nm, shp, F32, kind="ExternalOutput").ap()
        for nm, shp in [("d_yd", [DI, TOK]), ("d_gd", [DM, TOK]),
                        ("d_gated", [DI, TOK])]:
            dbg[nm] = DT(nm, shp, BF16, kind="ExternalOutput").ap()
        dbg["d_y2b"] = DT("d_y2b", [DI, TOK], F32, kind="ExternalOutput").ap()

    ii = np.arange(128)
    Unp = (ii[:, None] <= ii[None, :]).astype(BF)
    c_U = nc.inline_tensor(Unp, "cU")
    c_U2 = nc.inline_tensor(np.tile(Unp, (1, 2)).astype(BF), "cU2")
    c_U4 = nc.inline_tensor(np.tile(Unp, (1, 4)).astype(BF), "cU4")
    c_Us = nc.inline_tensor((ii[:, None] > ii[None, :]).astype(BF), "cUs")
    c_ao = nc.inline_tensor(np.ones((128, 128), dtype=BF), "cAO")
    c_m01 = nc.inline_tensor((ii[None, :] >= ii[:, None]).astype(BF), "cM01")
    c_idb = nc.inline_tensor(np.eye(128, dtype=BF), "cIdb")
    c_idf = nc.inline_tensor(np.eye(128, dtype=np.float32), "cIdf")
    c_of = nc.inline_tensor(np.ones((1, 128), dtype=np.float32), "cOF")
    dm8 = np.zeros((8, 8, 128), dtype=BF)
    for m in range(8):
        dm8[m, m, :] = 1
    c_dm8 = nc.inline_tensor(dm8.reshape(8, 1024), "cDM8")
    c_ao8 = nc.inline_tensor(np.ones((8, 128), dtype=BF), "cAO8")

    sb = nc.alloc_sbuf_tensor
    t_U = sb("tU", [128, 128], BF16)
    t_U2 = sb("tU2", [128, 256], BF16)
    t_U4 = sb("tU4", [128, 512], BF16)
    t_Us = sb("tUs", [128, 128], BF16)
    t_ao = sb("tAO", [128, 128], BF16)
    t_m01 = sb("tM01", [128, 128], BF16)
    t_idb = sb("tIdb", [128, 128], BF16)
    t_idf = sb("tIdf", [128, 128], F32)
    t_of = sb("tOF", [1, 128], F32)
    t_dm8 = sb("tDM8", [8, 1024], BF16)
    t_ao8 = sb("tAO8", [8, 128], BF16)
    t_nw = sb("tNW", [128, NCD], F32)
    t_cw = sb("tCW", [128, 320], F32)
    t_cb = sb("tCB", [128, 80], F32)
    t_dtb = sb("tDTB", [128, 1], F32)
    t_aneg = sb("tANEG", [128, 1], F32)
    t_dprow = sb("tDPROW", [1, 128], F32)
    t_dpb = sb("tDPB", [128, 128], F32)
    t_gnw = sb("tGNW", [128, 64], F32)
    t_dpoc = sb("tDPOC", [128, 64], F32)
    t_g16 = sb("tG16", [1, 16], F32)
    t_gbc = sb("tGBC", [128, 16], F32)
    t_selt = sb("tSELT", [8, 8], F32)
    t_maskv = sb("tMASKV", [8, 1], F32)
    t_ffnb = sb("tFFNB", [128, NCD], F32)
    t_mu = sb("tMU", [128, THL], F32)
    t_c2 = sb("tC2", [128, THL], F32)
    t_Lc = sb("tLC", [128, 1], F32)

    CONSTLOADS = [
        (t_U, c_U.ap()), (t_U2, c_U2.ap()), (t_U4, c_U4.ap()), (t_Us, c_Us.ap()),
        (t_ao, c_ao.ap()), (t_m01, c_m01.ap()), (t_idb, c_idb.ap()),
        (t_idf, c_idf.ap()), (t_of, c_of.ap()), (t_dm8, c_dm8.ap()),
        (t_ao8, c_ao8.ap()), (t_nw, nw_in), (t_cw, cw_in), (t_cb, cb_in),
        (t_dtb, dtb_in), (t_dprow, dp_in), (t_gnw, gnw_in), (t_dpoc, dpoc_in),
        (t_g16, g16_in),
        (t_selt, selt_in), (t_maskv, maskv_in), (t_ffnb, ffnb_in),
    ]

    MID = contextlib.ExitStack()
    with MID:
        t_afT = MID.enter_context(nc.sbuf_tensor("tAFT", [128, NCH * 128], BF16))
        t_dtT = MID.enter_context(nc.sbuf_tensor("tDTT", [128, NCH * 128], F32))
        t_dsT = MID.enter_context(nc.sbuf_tensor("tDST", [128, NCH * 128], F32))
        t_dcb = MID.enter_context(nc.sbuf_tensor("tDCB", [128, NCH * 128], F32))
        t_Btm = MID.enter_context(nc.sbuf_tensor("tBTM", [128, 64 * 128], BF16))
        t_STm = MID.enter_context(nc.sbuf_tensor("tSTM", [128, 64 * 128], BF16))
        XS = contextlib.ExitStack()
        x_cn = XS.enter_context(nc.sbuf_tensor("x_cn", [128, NCD * THL], BF16))
        t_ct = XS.enter_context(nc.sbuf_tensor("t_ct", [128, THL], F32))
        if True:
            ES1 = contextlib.ExitStack()
            with ES1:
                p0x = ES1.enter_context(nc.sbuf_tensor("p0x", [128, 2 * THL], F32))
                p0xb = ES1.enter_context(nc.sbuf_tensor("p0xb", [128, 2 * THL], BF16))
                p0xq = ES1.enter_context(nc.sbuf_tensor("p0xq", [128, 2 * THL], BF16))
                p0s1 = ES1.enter_context(nc.sbuf_tensor("p0s1", [128, THL], F32))
                p0s2 = ES1.enter_context(nc.sbuf_tensor("p0s2", [128, THL], F32))
                p0s3 = ES1.enter_context(nc.sbuf_tensor("p0s3", [128, THL], F32))
                psA0 = ES1.enter_context(nc.psum_tensor("psA0", [128, 512], F32))
                psA1 = ES1.enter_context(nc.psum_tensor("psA1", [128, 512], F32))
                psA2 = ES1.enter_context(nc.psum_tensor("psA2", [128, 512], F32))
                psB0 = ES1.enter_context(nc.psum_tensor("psB0", [128, 512], F32))
                psB1 = ES1.enter_context(nc.psum_tensor("psB1", [128, 512], F32))
                psB2 = ES1.enter_context(nc.psum_tensor("psB2", [128, 512], F32))
                psR = ES1.enter_context(nc.psum_tensor("psR", [128, 128], F32))
                blk = ES1.enter_context(nc.Block("ph0"))
                psA = [psA0, psA1, psA2]
                psB = [psB0, psB1, psB2]

                def _(sy):
                    for dst, src in CONSTLOADS:
                        tk.inc(sy.dma_start(dst[:, :], src), "sy", "c0", 16)
                    tk.inc(sy.dma_start(p0s1[0:128, 0:1], alog_in), "sy", "c0", 16)
                blk.sync(_)

                def _(s):
                    tk.wait(s, "c0")
                    s.activation(t_aneg[:, :], p0s1[0:128, 0:1], AF.Exp)
                    tk.inc(s.activation(t_aneg[:, :], t_aneg[:, :], AF.Copy, scale=-1.0),
                           "s", "aneg")
                blk.scalar(_)

                for d in range(NCD):
                    par = d % 2

                    def _(sy, d=d, par=par):
                        if d == 0:
                            tk.wait(sy, "aneg")
                        if d >= 2:
                            tk.wait(sy, f"p0cast{d - 2}")
                        tk.inc(sy.dma_start(p0x[:, par * THL:(par + 1) * THL],
                                            xfm[128 * d:128 * (d + 1), :]), f"fx{par}", f"p0x{d}", 16)
                    blk.sync(_)

                    def _(v, d=d, par=par):
                        tk.wait(v, f"p0x{d}")
                        if d >= 2:
                            tk.wait(v, f"p0mm{d - 2}")
                        tk.inc(v.tensor_copy(p0xb[:, par * THL:(par + 1) * THL],
                                             p0x[:, par * THL:(par + 1) * THL]),
                               "v", f"p0cast{d}")
                    blk.vector(_)

                    def _(s, d=d, par=par):
                        tk.wait(s, f"p0cast{d}")
                        if d >= 2:
                            tk.wait(s, f"p0mm{d - 2}")
                        tk.inc(s.activation(p0xq[:, par * THL:(par + 1) * THL],
                                            p0xb[:, par * THL:(par + 1) * THL], AF.Square),
                               "s", f"p0sq{d}")
                    blk.scalar(_)

                    def _(t, d=d, par=par):
                        tk.wait(t, f"p0sq{d}")
                        for j in range(3):
                            sl = slice(par * THL + TT0[j], par * THL + TT0[j] + TTW[j])
                            t.matmul(psA[j][:, 0:TTW[j]], t_ao[:, :], p0xb[:, sl],
                                     start=(d == 0), stop=(d == NCD - 1))
                            mm = t.matmul(psB[j][:, 0:TTW[j]], t_ao[:, :], p0xq[:, sl],
                                          start=(d == 0), stop=(d == NCD - 1))
                        tk.inc(mm, "t", f"p0mm{d}")
                    blk.tensor(_)

                def _(t):
                    tk.inc(t.matmul(psR[:, :], t_of[:, :], t_dprow[:, :],
                                    start=True, stop=True), "t", "dpmm")
                blk.tensor(_)

                def _(v):
                    tk.wait(v, f"p0mm{NCD - 1}")
                    for j in range(3):
                        v.tensor_scalar(t_mu[:, TT0[j]:TT0[j] + TTW[j]], psA[j][:, 0:TTW[j]],
                                        1.0 / DM, None, ALU.mult)
                        v.tensor_scalar(p0s2[:, TT0[j]:TT0[j] + TTW[j]], psB[j][:, 0:TTW[j]],
                                        1.0 / DM, None, ALU.mult)
                    v.tensor_mul(p0s3[:, :], t_mu[:, :], t_mu[:, :])
                    v.tensor_sub(p0s2[:, :], p0s2[:, :], p0s3[:, :])
                    v.tensor_scalar(p0s3[:, :], p0s2[:, :], 1.0 + EPS, EPS * EPS, ALU.mult, ALU.add)
                    tk.inc(v.tensor_scalar(p0s2[:, :], p0s2[:, :], 1.0, EPS, ALU.mult, ALU.add),
                           "v", "rsargs")
                    tk.wait(v, "dpmm")
                    tk.inc(v.tensor_copy(t_dpb[:, :], psR[:, :]), "v", "dpbc")
                blk.vector(_)

                def _(s):
                    tk.wait(s, "rsargs")
                    s.activation(t_ct[:, :], p0s3[:, :], AF.Sqrt)
                    tk.inc(s.activation(t_c2[:, :], p0s2[:, :], AF.Sqrt),
                           "s", "rsraw")
                blk.scalar(_)

                def _(v):
                    tk.wait(v, "rsraw")
                    v.reciprocal(t_ct[:, :], t_ct[:, :])
                    v.reciprocal(t_c2[:, :], t_c2[:, :])
                    for arg, y in ((p0s3, t_ct), (p0s2, t_c2)):
                        v.tensor_mul(p0s1[:, :], y[:, :], y[:, :])
                        v.tensor_mul(p0s1[:, :], p0s1[:, :], arg[:, :])
                        v.tensor_scalar(p0s1[:, :], p0s1[:, :], -0.5, 1.5, ALU.mult, ALU.add)
                        v.tensor_mul(y[:, :], y[:, :], p0s1[:, :])
                    tk.inc(v.engine_nop(), "v", "ctdone")
                blk.vector(_)

                for d in range(NCD):
                    par = d % 2

                    def _(sy, d=d, par=par):
                        if d == 0:
                            tk.wait(sy, "ctdone")
                        if d >= 2:
                            tk.wait(sy, f"xcn{d - 2}")
                        tk.inc(sy.dma_start(p0x[:, par * THL:(par + 1) * THL],
                                            xfm[128 * d:128 * (d + 1), :]), f"fx{par}", f"p0y{d}", 16)
                    blk.sync(_)

                    def _(v, d=d, par=par):
                        tk.wait(v, f"p0y{d}")
                        v.tensor_sub(p0s1[:, :], p0x[:, par * THL:(par + 1) * THL], t_mu[:, :])
                        tk.inc(v.tensor_scalar(x_cn[:, d * THL:(d + 1) * THL], p0s1[:, :],
                                               t_nw[:, d:d + 1], None, ALU.mult), "v", f"xcn{d}")
                    blk.vector(_)

                if DEBUG:
                    def _(g):
                        tk.wait(g, f"xcn{NCD - 1}")
                        xv = dbg["d_xcn"].rearrange("(c p) t -> c p t", p=128)
                        for d in range(NCD):
                            tk.inc(g.dma_start(xv[d], x_cn[:, d * THL:(d + 1) * THL]), "g", "dbg1", 16)
                        tk.inc(g.dma_start(dbg["d_mu"], t_mu[:, :]), "g", "dbg2", 16)
                        tk.inc(g.dma_start(dbg["d_ct"], t_ct[:, :]), "g", "dbg3", 16)
                    blk.gpsimd(_)

            if STOP == "p0":
                XS.close()
                return nc
            ES2 = contextlib.ExitStack()
            with ES2:
                wdt = ES2.enter_context(nc.sbuf_tensor("wdt", [128, NCD * 128], BF16))
                pj0 = ES2.enter_context(nc.sbuf_tensor("pj0", [128, THL], F32))
                dtsb = ES2.enter_context(nc.sbuf_tensor("dtsb", [128, THL], F32))
                afsb = ES2.enter_context(nc.sbuf_tensor("afsb", [128, THL], F32))
                acssb = ES2.enter_context(nc.sbuf_tensor("acssb", [128, 128], F32))
                dssb = ES2.enter_context(nc.sbuf_tensor("dssb", [128, 128], F32))
                lastc = ES2.enter_context(nc.sbuf_tensor("lastc", [128, 1], F32))
                pjA = ES2.enter_context(nc.psum_tensor("pjA", [128, 512], F32))
                pjB = ES2.enter_context(nc.psum_tensor("pjB", [128, 512], F32))
                pjC = ES2.enter_context(nc.psum_tensor("pjC", [128, 512], F32))
                pt0 = ES2.enter_context(nc.psum_tensor("pt0", [128, 128], F32))
                pt1 = ES2.enter_context(nc.psum_tensor("pt1", [128, 128], F32))
                blk = ES2.enter_context(nc.Block("ph1dt"))
                pj = [pjA, pjB, pjC]

                def _(g):
                    tk.inc(g.dma_start(wdt[:, :], WinT[OC_DT]), "wdt", "wdt", 16)
                blk.gpsimd(_)

                def _(t):
                    tk.wait(t, "wdt")
                    for d in range(NCD):
                        for j in range(3):
                            mm = t.matmul(pj[j][:, 0:TTW[j]], wdt[:, d * 128:(d + 1) * 128],
                                          x_cn[:, d * THL + TT0[j]:d * THL + TT0[j] + TTW[j]],
                                          start=(d == 0), stop=(d == NCD - 1))
                    tk.inc(mm, "t", "dtmm")
                blk.tensor(_)

                def _(v):
                    tk.wait(v, "dtmm")
                    for j in range(3):
                        v.tensor_mul(pj0[:, TT0[j]:TT0[j] + TTW[j]], pj[j][:, 0:TTW[j]],
                                     t_ct[:, TT0[j]:TT0[j] + TTW[j]])
                    tk.inc(v.engine_nop(), "v", "dtevac")
                blk.vector(_)

                def _(s):
                    tk.wait(s, "dtevac")
                    s.activation(dtsb[:, :], pj0[:, :], AF.Abs, bias=t_dtb[:, :])
                    s.activation(dtsb[:, :], dtsb[:, :], AF.Exp, scale=-1.0)
                    s.activation(dtsb[:, :], dtsb[:, :], AF.Ln, bias=1.0)
                    tk.inc(s.activation(afsb[:, :], pj0[:, :], AF.Relu, bias=t_dtb[:, :]),
                           "s", "sfp")
                blk.scalar(_)

                def _(v):
                    tk.wait(v, "sfp")
                    v.tensor_add(dtsb[:, :], afsb[:, :], dtsb[:, :])
                    v.memset(t_Lc[:, :], 0.0)
                    tk.inc(v.tensor_scalar(afsb[:, :], dtsb[:, :], t_aneg[:, :], None, ALU.mult),
                           "v", "afull")
                blk.vector(_)

                for c in range(NCH):
                    sl = slice(3 + 128 * c, 3 + 128 * (c + 1))
                    csl = slice(128 * c, 128 * (c + 1))

                    def _(t, c=c, sl=sl):
                        tk.wait(t, "afull")
                        if c >= 1:
                            tk.wait(t, f"adsT_{c - 1}")
                        t.transpose(pt0[:, :], dtsb[:, sl], t_idf[:, :])
                        tk.inc(t.transpose(pt1[:, :], afsb[:, sl], t_idf[:, :]), "t", f"atr_{c}")
                    blk.tensor(_)

                    def _(v, c=c, csl=csl):
                        tk.wait(v, f"atr_{c}")
                        v.tensor_copy(t_dtT[:, csl], pt0[:, :])
                        tk.inc(v.tensor_copy(t_afT[:, csl], pt1[:, :]), "v", f"aev_{c}")
                    blk.vector(_)

                    def _(t, c=c, csl=csl):
                        tk.wait(t, f"aev_{c}")
                        t.matmul(pt0[:, :], t_afT[:, csl], t_U[:, :], start=True, stop=True)
                        tk.inc(t.matmul(pt1[:, :], t_ao[:, :], t_afT[:, csl],
                                        start=True, stop=True), "t", f"amm_{c}")
                    blk.tensor(_)

                    def _(v, c=c):
                        tk.wait(v, f"amm_{c}")
                        v.tensor_copy(acssb[:, :], pt0[:, :])
                        v.tensor_copy(lastc[:, :], pt0[:, 127:128])
                        tk.inc(v.tensor_add(t_Lc[:, :], t_Lc[:, :], lastc[:, :]), "v", f"aacs_{c}")
                    blk.vector(_)

                    def _(s, c=c, csl=csl):
                        tk.wait(s, f"aacs_{c}")
                        s.activation(dssb[:, :], acssb[:, :], AF.Exp, scale=-1.0, bias=lastc[:, :])
                        tk.inc(s.activation(t_dcb[:, csl], pt1[:, :], AF.Exp), "s", f"aexp_{c}")
                    blk.scalar(_)

                    def _(t, c=c):
                        tk.wait(t, f"aexp_{c}")
                        tk.inc(t.transpose(pt0[:, :], dssb[:, :], t_idf[:, :]), "t", f"atr2_{c}")
                    blk.tensor(_)

                    def _(v, c=c, csl=csl):
                        tk.wait(v, f"atr2_{c}")
                        tk.inc(v.tensor_copy(t_dsT[:, csl], pt0[:, :]), "v", f"adsT_{c}")
                    blk.vector(_)

                if DEBUG:
                    def _(g):
                        tk.wait(g, f"adsT_{NCH - 1}")
                        tk.inc(g.dma_start(dbg["d_dt"], dtsb[:, :]), "g", "dbg4", 16)
                    blk.gpsimd(_)

            if STOP == "p1dt":
                XS.close()
                return nc
            ES3 = contextlib.ExitStack()
            with ES3:
                wsl = ES3.enter_context(nc.sbuf_tensor("wsl", [128, 2 * NCD * 128], BF16))
                pjs = ES3.enter_context(nc.sbuf_tensor("pjs", [128, 2 * THL], F32))
                cacc = ES3.enter_context(nc.sbuf_tensor("cacc", [128, 2 * TOK], F32))
                Bfm = ES3.enter_context(nc.sbuf_tensor("Bfm", [128, 8 * TOK], BF16))
                stg = ES3.enter_context(nc.sbuf_tensor("stg", [128, 2 * TOK], BF16))
                stgT = ES3.enter_context(nc.sbuf_tensor("stgT", [128, 2 * TOK], BF16))
                qjA0 = ES3.enter_context(nc.psum_tensor("qjA0", [128, 512], F32))
                qjB0 = ES3.enter_context(nc.psum_tensor("qjB0", [128, 512], F32))
                qjC0 = ES3.enter_context(nc.psum_tensor("qjC0", [128, 512], F32))
                qjA1 = ES3.enter_context(nc.psum_tensor("qjA1", [128, 512], F32))
                qjB1 = ES3.enter_context(nc.psum_tensor("qjB1", [128, 512], F32))
                qjC1 = ES3.enter_context(nc.psum_tensor("qjC1", [128, 512], F32))
                qtb0 = ES3.enter_context(nc.psum_tensor("qtb0", [128, 128], F32))
                qtb1 = ES3.enter_context(nc.psum_tensor("qtb1", [128, 128], F32))
                blk = ES3.enter_context(nc.Block("ph1m"))
                qj2 = [[qjA0, qjB0, qjC0], [qjA1, qjB1, qjC1]]
                qtb = [qtb0, qtb1]
                naux = [0]
                aux_last = {}
                NSEQ = len(OC_ORDER)

                def octype(oc):
                    if oc >= 136:
                        return "C"
                    if oc >= 128:
                        return "B"
                    if oc >= 64:
                        return "hid"
                    return "gate"

                def emit_slab(i):
                    oc = OC_ORDER[i]

                    def _(g, i=i, oc=oc):
                        if i >= 2:
                            tk.wait(g, f"mm_{i - 2}")
                        dst = wsl[:, (i % 2) * NCD * 128:(i % 2 + 1) * NCD * 128]
                        tk.inc(g.dma_start(dst, WinT[oc]), f"wsl{i % 2}", f"wsl_{i}", 16)
                    blk.gpsimd(_)

                def emit_mm(i):
                    def _(t, i=i):
                        tk.wait(t, f"wsl_{i}")
                        if i >= 2:
                            tk.wait(t, f"evac_{i - 2}")
                            if octype(OC_ORDER[i - 2]) == "C" and (i - 2) in aux_last:
                                tk.wait(t, f"vax_{aux_last[i - 2]}")
                        base = (i % 2) * NCD * 128
                        qj = qj2[i % 2]
                        nj = 3 if OC_ORDER[i] >= 64 else 2
                        for d in range(NCD):
                            for j in range(nj):
                                mm = t.matmul(qj[j][:, 0:TTW[j]],
                                              wsl[:, base + d * 128:base + (d + 1) * 128],
                                              x_cn[:, d * THL + TT0[j]:d * THL + TT0[j] + TTW[j]],
                                              start=(d == 0), stop=(d == NCD - 1))
                        tk.inc(mm, "t", f"mm_{i}")
                    blk.tensor(_)

                def emit_evac_conv(i):
                    oc = OC_ORDER[i]
                    ty = octype(oc)

                    def _(v, i=i, oc=oc, ty=ty):
                        pp = i % 2
                        tk.wait(v, f"mm_{i}")
                        if i >= 2:
                            tk.wait(v, f"sact_{i - 2}")
                        nj = 3 if oc >= 64 else 2
                        for j in range(nj):
                            v.tensor_mul(pjs[:, pp * THL + TT0[j]:pp * THL + TT0[j] + TTW[j]],
                                         qj2[i % 2][j][:, 0:TTW[j]], t_ct[:, TT0[j]:TT0[j] + TTW[j]])
                        tk.inc(v.engine_nop(), "v", f"evac_{i}")
                        if ty != "gate":
                            hc = oc - 64
                            base = pjs[:, pp * THL:(pp + 1) * THL]
                            ca = cacc[:, pp * TOK:(pp + 1) * TOK]
                            v.tensor_scalar(ca, base[:, 0:TOK], t_cw[:, 4 * hc:4 * hc + 1],
                                            None, ALU.mult)
                            for kk in range(1, 4):
                                v.scalar_tensor_tensor(ca, base[:, kk:kk + TOK],
                                                       t_cw[:, 4 * hc + kk:4 * hc + kk + 1],
                                                       ca, ALU.mult, ALU.add)
                        tk.inc(v.engine_nop(), "v", f"conv_{i}")
                    blk.vector(_)

                def emit_silu(i):
                    oc = OC_ORDER[i]
                    ty = octype(oc)

                    def _(s, i=i, oc=oc, ty=ty):
                        pp = i % 2
                        tk.wait(s, f"conv_{i}")
                        if i >= 2:
                            tk.wait(s, f"stgfree_{i - 2}")
                        if ty == "gate":
                            act = s.activation(stg[:, pp * TOK:(pp + 1) * TOK],
                                               pjs[:, pp * THL + 3:pp * THL + 3 + TOK], AF.Silu)
                        elif ty == "B":
                            bg = oc - 128
                            act = s.activation(Bfm[:, bg * TOK:(bg + 1) * TOK],
                                               cacc[:, pp * TOK:(pp + 1) * TOK], AF.Silu,
                                               bias=t_cb[:, (oc - 64):(oc - 64) + 1])
                        else:
                            act = s.activation(stg[:, pp * TOK:(pp + 1) * TOK],
                                               cacc[:, pp * TOK:(pp + 1) * TOK], AF.Silu,
                                               bias=t_cb[:, (oc - 64):(oc - 64) + 1])
                        tk.inc(act, "s", f"sact_{i}")
                    blk.scalar(_)

                def emit_aux(i):
                    oc = OC_ORDER[i]
                    ty = octype(oc)
                    if ty == "gate":
                        return
                    if ty == "C":
                        for c in range(NCH):
                            k = naux[0] + c

                            def _(t, i=i, oc=oc, c=c, k=k):
                                pp = i % 2
                                if c == 0:
                                    tk.wait(t, f"sact_{i}")
                                if k >= 2:
                                    tk.wait(t, f"vax_{k - 2}")
                                gg = oc - 136
                                mm = t.matmul(qtb[k % 2][:, :],
                                              Bfm[:, gg * TOK + 128 * c:gg * TOK + 128 * (c + 1)],
                                              stg[:, pp * TOK + 128 * c:pp * TOK + 128 * (c + 1)],
                                              start=True, stop=True)
                                tk.inc(mm, "t", f"tax_{k}")
                            blk.tensor(_)

                            def _(v, i=i, oc=oc, c=c, k=k):
                                tk.wait(v, f"tax_{k}")
                                gg = oc - 136
                                tk.inc(v.tensor_mul(t_STm[:, (gg * 8 + c) * 128:(gg * 8 + c + 1) * 128],
                                                    qtb[k % 2][:, :], t_m01[:, :]), "v", f"vax_{k}")
                            blk.vector(_)
                        naux[0] += NCH
                        aux_last[i] = naux[0] - 1
                    else:
                        def _(sy, i=i, oc=oc, ty=ty):
                            pp = i % 2
                            tk.wait(sy, f"sact_{i}")
                            for c in range(NCH):
                                if ty == "B":
                                    bg = oc - 128
                                    d = sy.dma_start_transpose(
                                        t_Btm[:, (bg * 8 + c) * 128:(bg * 8 + c + 1) * 128],
                                        Bfm[:, bg * TOK + 128 * c:bg * TOK + 128 * (c + 1)])
                                else:
                                    d = sy.dma_start_transpose(
                                        stgT[:, pp * TOK + 128 * c:pp * TOK + 128 * (c + 1)],
                                        stg[:, pp * TOK + 128 * c:pp * TOK + 128 * (c + 1)])
                                tk.inc(d, "trp", f"trp_{i}_{c}", 16)
                        blk.sync(_)

                def emit_store(i):
                    oc = OC_ORDER[i]
                    ty = octype(oc)
                    pp = i % 2

                    def _(sy, i=i, oc=oc, ty=ty, pp=pp):
                        if ty == "C":
                            gg = oc - 136
                            tk.wait(sy, f"sact_{i}")
                            tk.inc(sy.dma_start(C_d[128 * gg:128 * (gg + 1), :],
                                                stg[:, pp * TOK:(pp + 1) * TOK]), f"stg{pp}", f"st_{i}", 16)
                        elif ty == "gate":
                            tk.wait(sy, f"sact_{i}")
                            tk.inc(sy.dma_start(gate_d[128 * oc:128 * (oc + 1), :],
                                                stg[:, pp * TOK:(pp + 1) * TOK]), f"stg{pp}", f"st_{i}", 16)
                        elif ty == "hid":
                            ohc = oc - 64
                            tk.wait(sy, f"sact_{i}")
                            tk.inc(sy.dma_start(hidf_d[128 * ohc:128 * (ohc + 1), :],
                                                stg[:, pp * TOK:(pp + 1) * TOK]), f"stg{pp}", f"st_{i}a", 16)
                            tk.wait(sy, f"trp_{i}_{NCH - 1}")
                            dstT = hidt_d.rearrange("c p f -> p c f")[:, :, 128 * ohc:128 * (ohc + 1)]
                            tk.inc(sy.dma_start(dstT, stgT[:, pp * TOK:(pp + 1) * TOK]
                                                .rearrange("p (c f) -> p c f", f=128)),
                                    f"stgT{pp}", f"st_{i}", 16)
                        else:
                            tk.inc(sy.nop(), f"stg{pp}", f"st_{i}")
                    blk.sync(_)

                    def _(v, i=i, ty=ty):
                        if ty == "B":
                            tk.wait(v, f"trp_{i}_{NCH - 1}")
                        else:
                            tk.wait(v, f"st_{i}")
                            if ty == "hid":
                                tk.wait(v, f"st_{i}a")
                        tk.inc(v.engine_nop(), "v", f"stgfree_{i}")
                    blk.vector(_)

                for i in range(NSEQ):
                    emit_slab(i)
                    emit_mm(i)
                    if i >= 1:
                        emit_aux(i - 1)
                    emit_evac_conv(i)
                    emit_silu(i)
                    if i >= 1:
                        emit_store(i - 1)
                emit_aux(NSEQ - 1)
                emit_store(NSEQ - 1)

                def _(g):
                    tk.wait(g, "st_78a", "st_79a", "st_78", "st_79",
                            "st_142", "st_143", f"trp_{79}_{NCH - 1}")
                    tk.inc(g.engine_nop(), "gend", "p1st")
                blk.gpsimd(_)

        if STOP == "p1m":
            XS.close()
            return nc
        XS.close()
        ES4 = contextlib.ExitStack()
        with ES4:
            htc = ES4.enter_context(nc.sbuf_tensor("htc", [128, 2 * DI], BF16))
            hdt = ES4.enter_context(nc.sbuf_tensor("hdt", [128, 2 * 128], BF16))
            dtds = ES4.enter_context(nc.sbuf_tensor("dtds", [128, 128], F32))
            scst = ES4.enter_context(nc.sbuf_tensor("scst", [128, H * P], BF16))
            sloc = ES4.enter_context(nc.sbuf_tensor("sloc", [128, H * P], F32))
            psS0 = ES4.enter_context(nc.psum_tensor("psS0", [128, 128], F32))
            psS1 = ES4.enter_context(nc.psum_tensor("psS1", [128, 128], F32))
            blk = ES4.enter_context(nc.Block("ph2a"))
            psS = [psS0, psS1]

            def _(v):
                v.memset(sloc[:, :], 0.0)
                tk.inc(v.engine_nop(), "v", "slocinit")
            blk.vector(_)

            for c in range(NCH):
                par = c % 2

                def _(sy, c=c, par=par):
                    if c == 0:
                        tk.wait(sy, "p1st")
                    if c >= 2:
                        tk.wait(sy, f"2ats_{c - 2}_{63}")
                    tk.inc(sy.dma_start(htc[:, par * DI:(par + 1) * DI], hidt_d[c]),
                           f"htc{par}", f"htc_{c}", 16)
                blk.sync(_)

                def emit_smm(t, c, pr):
                    bpar = pr % 2
                    h0 = 2 * pr
                    k = c * 64 + pr
                    tk.wait(t, f"2ats_{c}_{pr}")
                    if k >= 2:
                        tk.wait(t, f"2aev_{(k - 2) // 64}_{(k - 2) % 64}")
                    gg = h0 // 16
                    for hp in range(2):
                        mm = t.matmul(psS[pr % 2][:, 64 * hp:64 * (hp + 1)],
                                      t_Btm[:, (gg * 8 + c) * 128:(gg * 8 + c + 1) * 128],
                                      hdt[:, bpar * 128 + 64 * hp:bpar * 128 + 64 * (hp + 1)],
                                      start=True, stop=True)
                    tk.bind(mm, f"2amm_{c}_{pr}")

                def emit_sev(v, c, pr):
                    tk.wait(v, f"2amm_{c}_{pr}")
                    if c >= 1 and pr == 0:
                        tk.wait(v, f"sst_{c - 1}")
                    tk.bind(v.tensor_copy(scst[:, 128 * pr:128 * (pr + 1)], psS[pr % 2][:, :]),
                            f"2aev_{c}_{pr}")

                def emit_sts(v, c, pr, par):
                    bpar = pr % 2
                    h0 = 2 * pr
                    k = c * 64 + pr
                    tk.wait(v, f"htc_{c}")
                    if k >= 2:
                        tk.wait(v, f"2amm_{(k - 2) // 64}_{(k - 2) % 64}")
                    op = v.tensor_tensor(hdt[:, bpar * 128:(bpar + 1) * 128].rearrange("p (a f) -> p a f", f=64),
                                         htc[:, par * DI + 128 * pr:par * DI + 128 * (pr + 1)]
                                         .rearrange("p (a f) -> p a f", f=64),
                                         dtds[:, h0:h0 + 2].unsqueeze(2).broadcast_to([128, 2, 64]),
                                         ALU.mult)
                    tk.bind(op, f"2ats_{c}_{pr}")

                tk.reserve("v", f"dtds_{c}")
                for i in range(65):
                    if i < 64:
                        tk.reserve("v", f"2ats_{c}_{i}")
                    if i >= 1:
                        tk.reserve("v", f"2aev_{c}_{i - 1}")
                for i in range(1, 65):
                    tk.reserve("t", f"2amm_{c}_{i - 1}")

                def _(v, c=c, par=par):
                    if c >= 1:
                        tk.wait(v, f"2amm_{c - 1}_{63}")
                    tk.bind(v.tensor_mul(dtds[:, :], t_dtT[:, 128 * c:128 * (c + 1)],
                                         t_dsT[:, 128 * c:128 * (c + 1)]), f"dtds_{c}")
                    for i in range(65):
                        if i < 64:
                            emit_sts(v, c, i, par)
                        if i >= 1:
                            emit_sev(v, c, i - 1)
                blk.vector(_)

                def _(t, c=c):
                    for i in range(1, 65):
                        emit_smm(t, c, i - 1)
                blk.tensor(_)

                def _(v, c=c):
                    tk.wait(v, f"2aev_{c}_{63}", "slocinit")
                    dv = t_dcb[:, 128 * c:128 * (c + 1)].unsqueeze(2).broadcast_to([128, 128, 64])
                    v.tensor_tensor(sloc[:, :].rearrange("p (h q) -> p h q", q=64),
                                    sloc[:, :].rearrange("p (h q) -> p h q", q=64), dv, ALU.mult)
                    tk.inc(v.tensor_add(sloc[:, :], sloc[:, :], scst[:, :]), "v", f"scan_{c}")
                blk.vector(_)

                def _(g, c=c):
                    tk.wait(g, f"scan_{c}")
                    tk.inc(g.dma_start(s_d[c], scst[:, :]), "sst", f"sst_{c}", 16)
                blk.gpsimd(_)

            def _(v):
                tk.wait(v, f"sst_{NCH - 1}")
                tk.inc(v.tensor_copy(scst[:, :], sloc[:, :]), "v", "slocbf")
            blk.vector(_)

            def _(g):
                tk.wait(g, "slocbf")
                tk.inc(g.dma_start(ccSi, scst[:, :]), "ccst", "ccsta", 16)
                tk.inc(g.dma_start(ccLi, t_Lc[:, :]), "ccst", "ccstb", 16)
                if DEBUG:
                    tk.inc(g.dma_start(dbg["d_sloc"], sloc[:, :]), "g", "dbg5", 16)
            blk.gpsimd(_)

        if STOP == "p2a":
            XS.close()
            return nc
        ES5 = contextlib.ExitStack()
        with ES5:
            xn = ES5.enter_context(nc.sbuf_tensor("xn", [128, NCD * TOK], BF16))
            fx = ES5.enter_context(nc.sbuf_tensor("fx", [128, THL], F32))
            fw = ES5.enter_context(nc.sbuf_tensor("fw", [128, 2 * NCD * 128], BF16))
            fwf = ES5.enter_context(nc.sbuf_tensor("fwf", [128, NCD * 128], F32))
            lasf = ES5.enter_context(nc.sbuf_tensor("lasf", [128, NCD * 32], F32))
            lbuf = ES5.enter_context(nc.sbuf_tensor("lbuf", [128, 1024], F32))
            laf = ES5.enter_context(nc.sbuf_tensor("laf", [128, 1024], F32))
            lbf = ES5.enter_context(nc.sbuf_tensor("lbf", [128, 1024], F32))
            labf = ES5.enter_context(nc.sbuf_tensor("labf", [128, 1024], BF16))
            lbbf = ES5.enter_context(nc.sbuf_tensor("lbbf", [32, 4096], BF16))
            lasbf = ES5.enter_context(nc.sbuf_tensor("lasbf", [128, NCD * 32], BF16))
            lbsbf = ES5.enter_context(nc.sbuf_tensor("lbsbf", [32, 4096], BF16))
            axb = ES5.enter_context(nc.sbuf_tensor("axb", [32, 2 * 1024], BF16))
            gst = ES5.enter_context(nc.sbuf_tensor("gst", [128, 2 * 1024], BF16))
            fps0 = ES5.enter_context(nc.psum_tensor("fps0", [128, 512], F32))
            fps1 = ES5.enter_context(nc.psum_tensor("fps1", [128, 512], F32))
            fpa0 = ES5.enter_context(nc.psum_tensor("fpa0", [32, 512], F32))
            fpa1 = ES5.enter_context(nc.psum_tensor("fpa1", [32, 512], F32))
            fpg = ES5.enter_context(nc.psum_tensor("fpg", [128, 16], F32))
            blk = ES5.enter_context(nc.Block("ffn"))
            fps = [fps0, fps1]
            fpa = [fpa0, fpa1]

            def _(g):
                tk.wait(g, "ccsta", "ccstb")
                tk.inc(g.collective_compute("AllGather", ALU.bypass,
                                            replica_groups=[list(range(8))],
                                            ins=[ccLi.opt()], outs=[ccLo.opt()]), "cc", "ccL")
                tk.inc(g.collective_compute("AllGather", ALU.bypass,
                                            replica_groups=[list(range(8))],
                                            ins=[ccSi.opt()], outs=[ccSo.opt()]), "cc", "ccS")
            blk.gpsimd(_)

            def _(t):
                tk.inc(t.matmul(fpg[:, :], t_of[:, :], t_g16[:, :], start=True, stop=True),
                       "t", "gmm")
            blk.tensor(_)

            def _(v):
                tk.wait(v, "gmm")
                tk.inc(v.tensor_copy(t_gbc[:, :], fpg[:, :]), "v", "gbc")
            blk.vector(_)

            for d in range(NCD):
                par = d % 2

                def _(sy, d=d, par=par):
                    if d >= 1:
                        tk.wait(sy, f"fxn_{d - 1}")
                    tk.inc(sy.dma_start(fx[:, :], xfm[128 * d:128 * (d + 1), :]),
                           "fx0", f"fx_{d}", 16)
                blk.sync(_)

                def _(v, d=d, par=par):
                    tk.wait(v, f"fx_{d}")
                    v.tensor_sub(fx[:, 3:3 + TOK], fx[:, 3:3 + TOK], t_mu[:, 3:3 + TOK])
                    tk.inc(v.tensor_mul(xn[:, d * TOK:(d + 1) * TOK], fx[:, 3:3 + TOK],
                                        t_c2[:, 3:3 + TOK]), "v", f"fxn_{d}")
                blk.vector(_)

            if STOP == "ffn_xn":
                return nc
            for which, acc in ((0, laf), (1, lbf)):
                src_d = lA_in if which == 0 else lB_in
                for n in range(16):
                    par = n % 2

                    def _(sy, which=which, n=n, src_d=src_d):
                        if n >= 1:
                            tk.wait(sy, f"mac{which}_{n - 1}")
                        elif which == 1:
                            tk.wait(sy, "mac0_15")
                        tk.inc(sy.dma_start(lbuf[:, :], src_d[n]),
                               "lbuf0", f"lbuf{which}_{n}", 16)
                    blk.sync(_)

                    def _(v, which=which, n=n, acc=acc):
                        tk.wait(v, f"lbuf{which}_{n}")
                        if which == 0 and n == 0:
                            tk.wait(v, "gbc")
                        if n == 0:
                            op = v.tensor_scalar(acc[:, :], lbuf[:, :],
                                                 t_gbc[:, n:n + 1], None, ALU.mult)
                        else:
                            op = v.scalar_tensor_tensor(acc[:, :], lbuf[:, :],
                                                        t_gbc[:, n:n + 1], acc[:, :], ALU.mult, ALU.add)
                        tk.inc(op, "v", f"mac{which}_{n}")
                    blk.vector(_)

            if STOP == "ffn_mac0":
                return nc
            def _(v):
                tk.wait(v, "mac0_15")
                tk.inc(v.tensor_copy(labf[:, :], laf[:, :]), "v", "labf")
            blk.vector(_)
            if STOP == "ffn_mac":
                return nc

            def _(sy):
                tk.inc(sy.dma_start(lasf[:, :], lAs_in), "lashf", "lashf", 16)
            blk.sync(_)

            def _(v):
                tk.wait(v, "lashf")
                tk.inc(v.tensor_copy(lasbf[:, :], lasf[:, :]), "v", "lash")
            blk.vector(_)

            def _(g):
                tk.wait(g, "mac1_15")
                for j in range(4):
                    tk.inc(g.dma_start(lbbf[:, 1024 * j:1024 * (j + 1)],
                                       lbf[32 * j:32 * (j + 1), :]), "lbsh", "lbsh", 16)
                for j in range(4):
                    tk.inc(g.dma_start(lbsbf[:, 1024 * j:1024 * (j + 1)],
                                       lBs_in[32 * j:32 * (j + 1), :]), "lbsh", "lbsh", 16)
            blk.gpsimd(_)

            if STOP == "ffn_sh":
                return nc
            AXSTOP = 1 if STOP == "ffn_ax0" else 2
            for tt in range(AXSTOP):
                def _(t, tt=tt):
                    tk.wait(t, "labf", "lash", f"fxn_{NCD - 1}")
                    if tt >= 1:
                        tk.wait(t, f"axev_{tt - 1}")
                    for d in range(NCD):
                        t.matmul(fpa[0][:, :], labf[:, 32 * d:32 * (d + 1)],
                                 xn[:, d * TOK + 512 * tt:d * TOK + 512 * (tt + 1)],
                                 start=(d == 0), stop=(d == NCD - 1))
                    for d in range(NCD):
                        mm = t.matmul(fpa[1][:, :], lasbf[:, 32 * d:32 * (d + 1)],
                                      xn[:, d * TOK + 512 * tt:d * TOK + 512 * (tt + 1)],
                                      start=(d == 0), stop=(d == NCD - 1))
                    tk.inc(mm, "t", f"axmm_{tt}")
                blk.tensor(_)

                def _(v, tt=tt):
                    tk.wait(v, f"axmm_{tt}")
                    v.tensor_copy(axb[0:32, tt * 1024:tt * 1024 + 512], fpa[0][:, :])
                    tk.inc(v.tensor_copy(axb[0:32, tt * 1024 + 512:(tt + 1) * 1024], fpa[1][:, :]),
                           "v", f"axev_{tt}")
                blk.vector(_)
            if STOP == "ffn_ax0":
                return nc

            if STOP == "ffn_ax":
                return nc
            for i in range(32):
                def _(sy, i=i):
                    if i >= 1:
                        tk.wait(sy, f"fwc_{i - 1}")
                    tk.inc(sy.dma_start(fwf[:, :], Wffn[i]), "fwf", f"fwf_{i}", 16)
                blk.sync(_)

                def _(v, i=i):
                    tk.wait(v, f"fwf_{i}")
                    if i >= 2:
                        tk.wait(v, f"fmm_{i - 2}_1")
                    base = (i % 2) * NCD * 128
                    tk.inc(v.tensor_copy(fw[:, base:base + NCD * 128], fwf[:, :]),
                           "v", f"fwc_{i}")
                blk.vector(_)

                for tt in range(2):
                    def _(t, i=i, tt=tt):
                        tk.wait(t, f"fwc_{i}", f"axev_{tt}", "lbsh")
                        if i >= 1:
                            tk.wait(t, f"fsact_{i - 1}_{tt}")
                        base = (i % 2) * NCD * 128
                        for d in range(NCD):
                            t.matmul(fps[tt][:, :], fw[:, base + 128 * d:base + 128 * (d + 1)],
                                     xn[:, d * TOK + 512 * tt:d * TOK + 512 * (tt + 1)],
                                     start=(d == 0), stop=False)
                        off = 1024 * (i // 8) + 128 * (i % 8)
                        t.matmul(fps[tt][:, :], lbbf[0:32, off:off + 128],
                                 axb[0:32, tt * 1024:tt * 1024 + 512], start=False, stop=False)
                        mm = t.matmul(fps[tt][:, :], lbsbf[0:32, off:off + 128],
                                      axb[0:32, tt * 1024 + 512:(tt + 1) * 1024],
                                      start=False, stop=True)
                        tk.inc(mm, "t", f"fmm_{i}_{tt}")
                    blk.tensor(_)

                    def _(s, i=i, tt=tt):
                        tk.wait(s, f"fmm_{i}_{tt}")
                        if i >= 2:
                            tk.wait(s, f"gstd_{i - 2}")
                        tk.inc(s.activation(gst[:, (i % 2) * 1024 + 512 * tt:(i % 2) * 1024 + 512 * (tt + 1)],
                                            fps[tt][:, :], AF.Silu, bias=t_ffnb[:, i:i + 1]),
                               "s", f"fsact_{i}_{tt}")
                    blk.scalar(_)

                def _(g, i=i):
                    tk.wait(g, f"fsact_{i}_1")
                    tk.inc(g.dma_start(g_d[128 * i:128 * (i + 1), :],
                                       gst[:, (i % 2) * 1024:(i % 2 + 1) * 1024]), f"gstd{i % 2}", f"gstd_{i}", 16)
                blk.gpsimd(_)

            if DEBUG:
                def _(g):
                    tk.wait(g, f"fxn_{NCD - 1}")
                    xv = dbg["d_xn"].rearrange("(c p) t -> c p t", p=128)
                    for d in range(NCD):
                        tk.inc(g.dma_start(xv[d], xn[:, d * TOK:(d + 1) * TOK]), "g", "dbg6", 16)
                    tk.wait(g, "dbg6")
                blk.gpsimd(_)

        if STOP == "ffn":
            XS.close()
            return nc
        ES6 = contextlib.ExitStack()
        with ES6:
            prev = ES6.enter_context(nc.sbuf_tensor("prev", [128, H * P], F32))
            prevb = ES6.enter_context(nc.sbuf_tensor("prevb", [128, H * P], BF16))
            sgm = ES6.enter_context(nc.sbuf_tensor("sgm", [128, H * P], BF16))
            htc2 = ES6.enter_context(nc.sbuf_tensor("htc2", [128, 2 * DI], BF16))
            Cc2 = ES6.enter_context(nc.sbuf_tensor("Cc2", [128, G * 128], BF16))
            ystg = ES6.enter_context(nc.sbuf_tensor("ystg", [128, 64 * 128], BF16))
            Lall = ES6.enter_context(nc.sbuf_tensor("Lall", [8, 128], F32))
            w8 = ES6.enter_context(nc.sbuf_tensor("w8", [8, 128], F32))
            wd8 = ES6.enter_context(nc.sbuf_tensor("wd8", [8, 1024], BF16))
            wb8 = ES6.enter_context(nc.sbuf_tensor("wb8", [128, 1024], F32))
            au4 = ES6.enter_context(nc.sbuf_tensor("au4", [128, 3 * 512], BF16))
            ebf = ES6.enter_context(nc.sbuf_tensor("ebf", [128, 3 * 512], BF16))
            eabf = ES6.enter_context(nc.sbuf_tensor("eabf", [128, 3 * 512], BF16))
            mbf = ES6.enter_context(nc.sbuf_tensor("mbf", [128, 3 * 512], BF16))
            mobf = ES6.enter_context(nc.sbuf_tensor("mobf", [128, 3 * 512], BF16))
            hdt2 = ES6.enter_context(nc.sbuf_tensor("hdt2", [128, 3 * 256], BF16))
            psE0 = ES6.enter_context(nc.psum_tensor("psE0", [128, 512], F32))
            psE1 = ES6.enter_context(nc.psum_tensor("psE1", [128, 512], F32))
            psEA0 = ES6.enter_context(nc.psum_tensor("psEA0", [128, 512], F32))
            psEA1 = ES6.enter_context(nc.psum_tensor("psEA1", [128, 512], F32))
            psY0 = ES6.enter_context(nc.psum_tensor("psY0", [128, 256], F32))
            psY1 = ES6.enter_context(nc.psum_tensor("psY1", [128, 256], F32))
            psW = ES6.enter_context(nc.psum_tensor("psW", [8, 128], F32))
            psWB = ES6.enter_context(nc.psum_tensor("psWB", [128, 512], F32))
            blk = ES6.enter_context(nc.Block("ph2b"))
            psY = [psY0, psY1]
            psE = [psE0, psE1]
            psEA = [psEA0, psEA1]

            def _(sy):
                tk.wait(sy, "ccL")
                tk.inc(sy.dma_start(Lall[:, :], ccLo.rearrange("(m h) o -> m (h o)", h=128)),
                       "lall", "lall", 16)
            blk.sync(_)

            def _(t):
                tk.wait(t, "lall")
                tk.inc(t.matmul(psW[:, :], t_selt[:, :], Lall[:, :], start=True, stop=True),
                       "t", "wmm")
            blk.tensor(_)

            def _(s):
                tk.wait(s, "wmm")
                tk.inc(s.activation(w8[:, :], psW[:, :], AF.Exp, bias=t_maskv[:, :]), "s", "wexp")
            blk.scalar(_)

            def _(v):
                tk.wait(v, "wexp")
                tk.inc(v.tensor_tensor(wd8[:, :].rearrange("p (m h) -> p m h", h=128),
                                       t_dm8[:, :].rearrange("p (m h) -> p m h", h=128),
                                       w8[:, :].unsqueeze(1).broadcast_to([8, 8, 128]), ALU.mult),
                       "v", "wd")
            blk.vector(_)

            def _(t):
                tk.wait(t, "wd")
                tk.inc(t.matmul(psWB[:, :], t_ao8[:, :], wd8[:, 0:512],
                                start=True, stop=True), "t", "wbmm0")
            blk.tensor(_)

            def _(v):
                tk.wait(v, "wbmm0")
                tk.inc(v.tensor_copy(wb8[:, 0:512], psWB[:, :]), "v", "wbev0")
            blk.vector(_)

            def _(t):
                tk.wait(t, "wbev0")
                tk.inc(t.matmul(psWB[:, :], t_ao8[:, :], wd8[:, 512:1024],
                                start=True, stop=True), "t", "wbmm1")
            blk.tensor(_)

            def _(v):
                tk.wait(v, "wbmm1")
                tk.inc(v.tensor_copy(wb8[:, 512:1024], psWB[:, :]), "v", "wb")
                v.memset(prev[:, :], 0.0)
                tk.inc(v.engine_nop(), "v", "previnit")
            blk.vector(_)

            for m in range(8):
                def _(sy, m=m):
                    tk.wait(sy, "ccS")
                    if m >= 1:
                        tk.wait(sy, f"sinadd_{m - 1}")
                    tk.inc(sy.dma_start(sgm[:, :], ccSo[128 * m:128 * (m + 1), :]),
                           "sgm", f"sgm_{m}", 16)
                blk.sync(_)

                def _(v, m=m):
                    tk.wait(v, f"sgm_{m}", "wb", "previnit")
                    v.tensor_tensor(sgm[:, :].rearrange("p (h q) -> p h q", q=64),
                                    sgm[:, :].rearrange("p (h q) -> p h q", q=64),
                                    wb8[:, 128 * m:128 * (m + 1)].unsqueeze(2).broadcast_to([128, 128, 64]),
                                    ALU.mult)
                    tk.inc(v.tensor_add(prev[:, :], prev[:, :], sgm[:, :]), "v", f"sinadd_{m}")
                blk.vector(_)

            if DEBUG:
                def _(g):
                    tk.wait(g, "sinadd_7")
                    tk.inc(g.dma_start(dbg["d_sin"], prev[:, :]), "g", "dbg7", 16)
                blk.gpsimd(_)

            def emit_au(v, c, s4):
                k = c * 32 + s4
                h0 = 4 * s4
                pr3 = k % 3
                tk.wait(v, f"htc2_{c}")
                if k >= 3:
                    tk.wait(v, f"2bmm_{(k - 3) // 32}_{(k - 3) % 32}")
                v.tensor_tensor(au4[:, pr3 * 512:(pr3 + 1) * 512].rearrange("p (a f) -> p a f", f=128),
                                t_U4[:, :].rearrange("p (a f) -> p a f", f=128),
                                t_afT[:, 128 * c + h0:128 * c + h0 + 4].unsqueeze(2).broadcast_to([128, 4, 128]),
                                ALU.mult)
                op = v.tensor_tensor(hdt2[:, pr3 * 256:(pr3 + 1) * 256].rearrange("p (a f) -> p a f", f=64),
                                     htc2[:, (c % 2) * DI + 64 * h0:(c % 2) * DI + 64 * h0 + 256]
                                     .rearrange("p (a f) -> p a f", f=64),
                                     t_dtT[:, 128 * c + h0:128 * c + h0 + 4].unsqueeze(2).broadcast_to([128, 4, 64]),
                                     ALU.mult)
                tk.bind(op, f"2bau_{c}_{s4}")

            def emit_emm(t, c, s4):
                k = c * 32 + s4
                par = s4 % 2
                pr3 = k % 3
                tk.wait(t, f"2bau_{c}_{s4}")
                if k >= 2:
                    tk.wait(t, f"2bexp_{(k - 2) // 32}_{(k - 2) % 32}")
                t.matmul(psE[par][:, :], t_Us[:, :], au4[:, pr3 * 512:(pr3 + 1) * 512],
                         start=True, stop=True)
                tk.bind(t.matmul(psEA[par][:, :], t_ao[:, :], au4[:, pr3 * 512:(pr3 + 1) * 512],
                                 start=True, stop=True), f"2bemm_{c}_{s4}")

            def emit_exp(s, c, s4):
                k = c * 32 + s4
                par = s4 % 2
                pr3 = k % 3
                tk.wait(s, f"2bemm_{c}_{s4}")
                if k >= 3:
                    tk.wait(s, f"2bmo_{(k - 3) // 32}_{(k - 3) % 32}")
                s.activation(ebf[:, pr3 * 512:(pr3 + 1) * 512], psE[par][:, :], AF.Exp)
                tk.bind(s.activation(eabf[:, pr3 * 512:(pr3 + 1) * 512], psEA[par][:, :], AF.Exp),
                        f"2bexp_{c}_{s4}")

            def emit_mo(v, c, s4):
                k = c * 32 + s4
                gg = (4 * s4) // 16
                pr3 = k % 3
                tk.wait(v, f"2bexp_{c}_{s4}", f"Cc2_{c}")
                v.tensor_tensor(mbf[:, pr3 * 512:(pr3 + 1) * 512].rearrange("p (a f) -> p a f", f=128),
                                ebf[:, pr3 * 512:(pr3 + 1) * 512].rearrange("p (a f) -> p a f", f=128),
                                t_STm[:, (gg * 8 + c) * 128:(gg * 8 + c + 1) * 128]
                                .unsqueeze(1).broadcast_to([128, 4, 128]), ALU.mult)
                tk.bind(v.tensor_tensor(mobf[:, pr3 * 512:(pr3 + 1) * 512].rearrange("p (a f) -> p a f", f=128),
                                        eabf[:, pr3 * 512:(pr3 + 1) * 512].rearrange("p (a f) -> p a f", f=128),
                                        Cc2[:, 128 * gg:128 * (gg + 1)].unsqueeze(1).broadcast_to([128, 4, 128]),
                                        ALU.mult), f"2bmo_{c}_{s4}")

            def emit_y(t, k):
                if k < 0:
                    return
                c, s4 = k // 32, k % 32
                h0 = 4 * s4
                par = s4 % 2
                pr3 = k % 3
                tk.wait(t, f"2bmo_{c}_{s4}", f"prevb_{c}")
                if k >= 2:
                    tk.wait(t, f"yev_{(k - 2) // 32}_{(k - 2) % 32}")
                for hp in range(4):
                    h = h0 + hp
                    r0 = 64 * (hp % 2)
                    c0 = 128 * (hp // 2)
                    t.matmul(psY[par][r0:r0 + 64, c0:c0 + 128],
                             hdt2[:, pr3 * 256 + 64 * hp:pr3 * 256 + 64 * (hp + 1)],
                             mbf[:, pr3 * 512 + 128 * hp:pr3 * 512 + 128 * (hp + 1)],
                             start=True, stop=False)
                    mm = t.matmul(psY[par][r0:r0 + 64, c0:c0 + 128],
                                  prevb[:, 64 * h:64 * (h + 1)],
                                  mobf[:, pr3 * 512 + 128 * hp:pr3 * 512 + 128 * (hp + 1)],
                                  start=False, stop=True)
                tk.bind(mm, f"2bmm_{c}_{s4}")

            def emit_yev(v, k):
                if k < 0:
                    return
                c, s4 = k // 32, k % 32
                par = s4 % 2
                tk.wait(v, f"2bmm_{c}_{s4}")
                if c >= 1 and s4 == 0:
                    tk.wait(v, f"yst_{c - 1}")
                tk.bind(v.tensor_copy(ystg[:, 256 * s4:256 * (s4 + 1)], psY[par][:, :]),
                        f"yev_{c}_{s4}")

            for c in range(NCH):
                def _(v, c=c):
                    tk.wait(v, "sinadd_7")
                    if c >= 1:
                        tk.wait(v, f"2bmm_{c - 1}_{31}", f"pupd_{c - 1}")
                    tk.inc(v.tensor_copy(prevb[:, :], prev[:, :]), "v", f"prevb_{c}")
                blk.vector(_)

                def _(sy, c=c):
                    if c == 0:
                        tk.wait(sy, "p1st")
                    if c >= 2:
                        tk.wait(sy, f"2bau_{c - 2}_{31}")
                    tk.inc(sy.dma_start(htc2[:, (c % 2) * DI:(c % 2 + 1) * DI], hidt_d[c]),
                           "htc2", f"htc2_{c}", 16)
                    if c >= 1:
                        tk.wait(sy, f"2bmo_{c - 1}_{31}")
                    tk.inc(sy.dma_start(Cc2[:, :].rearrange("p (g f) -> p g f", f=128),
                                        C_d.rearrange("(g p) t -> p g t", p=128)[:, :, 128 * c:128 * (c + 1)]),
                           "Cc2", f"Cc2_{c}", 16)
                    tk.wait(sy, "sinadd_7")
                    if c == 0:
                        tk.wait(sy, f"sst_{NCH - 1}")
                    if c >= 1:
                        tk.wait(sy, f"pupd_{c - 1}")
                    tk.inc(sy.dma_start(sgm[:, :], s_d[c]), "scld", f"scld_{c}", 16)
                blk.sync(_)

                for i in range(35):
                    if i < 32:
                        tk.reserve("v", f"2bau_{c}_{i}")
                    if 1 <= i <= 32:
                        tk.reserve("v", f"2bmo_{c}_{i - 1}")
                    if 3 <= i <= 34:
                        tk.reserve("v", f"yev_{c}_{i - 3}")
                for i in range(34):
                    if i < 32:
                        tk.reserve("t", f"2bemm_{c}_{i}")
                    if 2 <= i <= 33:
                        tk.reserve("t", f"2bmm_{c}_{i - 2}")
                for i in range(32):
                    tk.reserve("s", f"2bexp_{c}_{i}")

                def _(v, c=c):
                    for i in range(35):
                        if i < 32:
                            emit_au(v, c, i)
                        if 1 <= i <= 32:
                            emit_mo(v, c, i - 1)
                        if 3 <= i <= 34:
                            emit_yev(v, c * 32 + (i - 3))
                blk.vector(_)

                def _(t, c=c):
                    for i in range(34):
                        if i < 32:
                            emit_emm(t, c, i)
                        if 2 <= i <= 33:
                            emit_y(t, c * 32 + (i - 2))
                blk.tensor(_)

                def _(s, c=c):
                    for i in range(32):
                        emit_exp(s, c, i)
                blk.scalar(_)

                def _(v, c=c):
                    tk.wait(v, f"scld_{c}", f"2bmm_{c}_{31}")
                    dv = t_dcb[:, 128 * c:128 * (c + 1)].unsqueeze(2).broadcast_to([128, 128, 64])
                    v.tensor_tensor(prev[:, :].rearrange("p (h q) -> p h q", q=64),
                                    prev[:, :].rearrange("p (h q) -> p h q", q=64), dv, ALU.mult)
                    tk.inc(v.tensor_add(prev[:, :], prev[:, :], sgm[:, :]), "v", f"pupd_{c}")
                blk.vector(_)

                def _(g, c=c):
                    tk.wait(g, f"yev_{c}_{31}")
                    dstY = y_d.rearrange("(o p) t -> p o t", p=128)[:, :, 128 * c:128 * (c + 1)]
                    tk.inc(g.dma_start(dstY, ystg[:, :].rearrange("p (o f) -> p o f", f=128)),
                           "yst", f"yst_{c}", 16)
                blk.gpsimd(_)

    if STOP == "p2b":
        XS.close()
        return nc
    with nc.sbuf_tensor("y2", [128, 64 * TOK], BF16) as y2:
        ES7 = contextlib.ExitStack()
        with ES7:
            ygl = ES7.enter_context(nc.sbuf_tensor("ygl", [128, 8 * TOK], BF16))
            hld = ES7.enter_context(nc.sbuf_tensor("hld", [128, 2 * TOK], BF16))
            ggl = ES7.enter_context(nc.sbuf_tensor("ggl", [128, 2 * TOK], BF16))
            sqb = ES7.enter_context(nc.sbuf_tensor("sqb", [128, 2 * TOK], BF16))
            rsb = ES7.enter_context(nc.sbuf_tensor("rsb", [128, TOK], F32))
            rs0 = ES7.enter_context(nc.sbuf_tensor("rs0", [128, TOK], F32))
            rtmp = ES7.enter_context(nc.sbuf_tensor("rtmp", [128, TOK], F32))
            pg0 = ES7.enter_context(nc.psum_tensor("pg0", [128, 512], F32))
            pg1 = ES7.enter_context(nc.psum_tensor("pg1", [128, 512], F32))
            blk = ES7.enter_context(nc.Block("ph3"))
            for gg in range(8):
                for j in range(8):
                    oc = gg * 8 + j

                    def _(sy, gg=gg, j=j, oc=oc):
                        if oc == 0:
                            tk.wait(sy, f"yst_{NCH - 1}")
                        if gg >= 1 and j == 0:
                            tk.wait(sy, f"y2_{gg - 1}")
                        if oc >= 2:
                            tk.wait(sy, f"yg_{oc - 2}")
                        tk.inc(sy.dma_start(ygl[:, j * TOK:(j + 1) * TOK],
                                            y_d[128 * oc:128 * (oc + 1), :]), "p3y", f"p3y_{oc}", 16)
                        tk.inc(sy.dma_start(ggl[:, (oc % 2) * TOK:(oc % 2 + 1) * TOK],
                                            gate_d[128 * oc:128 * (oc + 1), :]), "p3g", f"p3g_{oc}", 16)
                        tk.inc(sy.dma_start(hld[:, (oc % 2) * TOK:(oc % 2 + 1) * TOK],
                                            hidf_d[128 * oc:128 * (oc + 1), :]), "p3h", f"p3h_{oc}", 16)
                    blk.sync(_)

                    def _(v, gg=gg, j=j, oc=oc):
                        tk.wait(v, f"p3y_{oc}", f"p3g_{oc}", f"p3h_{oc}")
                        v.scalar_tensor_tensor(ygl[:, j * TOK:(j + 1) * TOK],
                                               hld[:, (oc % 2) * TOK:(oc % 2 + 1) * TOK],
                                               t_dpoc[:, oc:oc + 1],
                                               ygl[:, j * TOK:(j + 1) * TOK], ALU.mult, ALU.add)
                        tk.inc(v.tensor_mul(ygl[:, j * TOK:(j + 1) * TOK],
                                            ygl[:, j * TOK:(j + 1) * TOK],
                                            ggl[:, (oc % 2) * TOK:(oc % 2 + 1) * TOK]),
                               "v", f"yg_{oc}")
                    blk.vector(_)

                    def _(s, gg=gg, j=j, oc=oc):
                        tk.wait(s, f"yg_{oc}")
                        if oc >= 2:
                            tk.wait(s, f"p3mm_{oc - 2}")
                        tk.inc(s.activation(sqb[:, (oc % 2) * TOK:(oc % 2 + 1) * TOK],
                                            ygl[:, j * TOK:(j + 1) * TOK], AF.Square),
                               "s", f"p3sq_{oc}")
                    blk.scalar(_)

                    def _(t, gg=gg, j=j, oc=oc):
                        tk.wait(t, f"p3sq_{oc}")
                        if j == 0 and gg >= 1:
                            tk.wait(t, f"rsarg_{gg - 1}")
                        t.matmul(pg0[:, :], t_ao[:, :], sqb[:, (oc % 2) * TOK:(oc % 2) * TOK + 512],
                                 start=(j == 0), stop=(j == 7))
                        mm = t.matmul(pg1[:, :], t_ao[:, :],
                                      sqb[:, (oc % 2) * TOK + 512:(oc % 2 + 1) * TOK],
                                      start=(j == 0), stop=(j == 7))
                        tk.inc(mm, "t", f"p3mm_{oc}")
                    blk.tensor(_)

                def _(v, gg=gg):
                    tk.wait(v, f"p3mm_{gg * 8 + 7}")
                    v.tensor_scalar(rsb[:, 0:512], pg0[:, :], 1.0 / 1024, EPS, ALU.mult, ALU.add)
                    tk.inc(v.tensor_scalar(rsb[:, 512:1024], pg1[:, :], 1.0 / 1024, EPS,
                                           ALU.mult, ALU.add), "v", f"rsarg_{gg}")
                blk.vector(_)

                def _(s, gg=gg):
                    tk.wait(s, f"rsarg_{gg}")
                    tk.inc(s.activation(rs0[:, :], rsb[:, :], AF.Sqrt),
                           "s", f"rsg_{gg}")
                blk.scalar(_)

                def _(v, gg=gg):
                    tk.wait(v, f"rsg_{gg}")
                    v.reciprocal(rs0[:, :], rs0[:, :])
                    v.tensor_mul(rtmp[:, :], rs0[:, :], rs0[:, :])
                    v.tensor_mul(rtmp[:, :], rtmp[:, :], rsb[:, :])
                    v.tensor_scalar(rtmp[:, :], rtmp[:, :], -0.5, 1.5, ALU.mult, ALU.add)
                    v.tensor_mul(rs0[:, :], rs0[:, :], rtmp[:, :])
                    for j in range(8):
                        oc = gg * 8 + j
                        v.tensor_mul(ygl[:, j * TOK:(j + 1) * TOK], ygl[:, j * TOK:(j + 1) * TOK],
                                     rs0[:, :])
                        v.tensor_scalar(y2[:, oc * TOK:(oc + 1) * TOK], ygl[:, j * TOK:(j + 1) * TOK],
                                        t_gnw[:, oc:oc + 1], None, ALU.mult)
                    tk.inc(v.engine_nop(), "v", f"y2_{gg}")
                blk.vector(_)

            if DEBUG:
                def _(g):
                    tk.wait(g, "y2_7")
                    yv = dbg["d_y2"].rearrange("(c p) t -> c p t", p=128)
                    for oc in range(64):
                        tk.inc(g.dma_start(yv[oc], y2[:, oc * TOK:(oc + 1) * TOK]), "g", "dbg8", 16)
                blk.gpsimd(_)

        ES8 = contextlib.ExitStack()
        with ES8:
            ow = ES8.enter_context(nc.sbuf_tensor("ow", [128, 2 * 64 * 128], BF16))
            x4 = ES8.enter_context(nc.sbuf_tensor("x4", [128, 2 * 512], F32))
            gg4 = ES8.enter_context(nc.sbuf_tensor("gg4", [128, 2 * 512], BF16))
            ot4 = ES8.enter_context(nc.sbuf_tensor("ot4", [128, 2 * 512], F32))
            sc4 = ES8.enter_context(nc.sbuf_tensor("sc4", [128, 512], F32))
            po0 = ES8.enter_context(nc.psum_tensor("po0", [128, 512], F32))
            po1 = ES8.enter_context(nc.psum_tensor("po1", [128, 512], F32))
            blk = ES8.enter_context(nc.Block("ph4"))
            po = [po0, po1]
            for oc in range(32):
                def _(g, oc=oc):
                    if oc >= 2:
                        tk.wait(g, f"4mm_{oc - 2}_1")
                    dst = ow[:, (oc % 2) * 64 * 128:(oc % 2 + 1) * 64 * 128]
                    tk.inc(g.dma_start(dst, WoutT[oc]), f"ow{oc % 2}", f"ow_{oc}", 16)
                blk.gpsimd(_)

                for tt in range(2):
                    kb = 2 * oc + tt

                    def _(sy, oc=oc, tt=tt, kb=kb):
                        if kb == 0:
                            tk.wait(sy, f"yst_{NCH - 1}")
                        if kb >= 2:
                            tk.wait(sy, f"4ev_{(kb - 2) // 2}_{(kb - 2) % 2}")
                        tk.inc(sy.dma_start(x4[:, (kb % 2) * 512:(kb % 2 + 1) * 512],
                                            xfm[128 * oc:128 * (oc + 1), 3 + 512 * tt:3 + 512 * (tt + 1)]),
                               f"x4{kb % 2}", f"4x_{oc}_{tt}", 16)
                        tk.inc(sy.dma_start(gg4[:, (kb % 2) * 512:(kb % 2 + 1) * 512],
                                            g_d[128 * oc:128 * (oc + 1), 512 * tt:512 * (tt + 1)]),
                               f"g4{kb % 2}", f"4g_{oc}_{tt}", 16)
                    blk.sync(_)

                    def _(t, oc=oc, tt=tt, kb=kb):
                        tk.wait(t, f"ow_{oc}")
                        if kb >= 2:
                            tk.wait(t, f"4ev_{(kb - 2) // 2}_{(kb - 2) % 2}")
                        base = (oc % 2) * 64 * 128
                        for ch in range(64):
                            mm = t.matmul(po[tt][:, :], ow[:, base + 128 * ch:base + 128 * (ch + 1)],
                                          y2[:, ch * TOK + 512 * tt:ch * TOK + 512 * (tt + 1)],
                                          start=(ch == 0), stop=(ch == 63))
                        tk.inc(mm, "t", f"4mm_{oc}_{tt}")
                    blk.tensor(_)

                    def _(v, oc=oc, tt=tt, kb=kb):
                        tk.wait(v, f"4mm_{oc}_{tt}", f"4x_{oc}_{tt}", f"4g_{oc}_{tt}")
                        if kb >= 2:
                            tk.wait(v, f"4st_{(kb - 2) // 2}_{(kb - 2) % 2}")
                        xt = x4[:, (kb % 2) * 512:(kb % 2 + 1) * 512]
                        v.tensor_sub(sc4[:, :], xt, t_mu[:, 3 + 512 * tt:3 + 512 * (tt + 1)])
                        v.tensor_mul(sc4[:, :], sc4[:, :], t_c2[:, 3 + 512 * tt:3 + 512 * (tt + 1)])
                        v.tensor_add(sc4[:, :], sc4[:, :], po[tt][:, :])
                        v.tensor_mul(sc4[:, :], sc4[:, :], gg4[:, (kb % 2) * 512:(kb % 2 + 1) * 512])
                        tk.inc(v.tensor_add(ot4[:, (kb % 2) * 512:(kb % 2 + 1) * 512], sc4[:, :], xt),
                               "v", f"4ev_{oc}_{tt}")
                    blk.vector(_)

                    def _(s, oc=oc, tt=tt, kb=kb):
                        tk.wait(s, f"4ev_{oc}_{tt}")
                        tk.inc(s.dma_start(out_fm[128 * oc:128 * (oc + 1), 512 * tt:512 * (tt + 1)],
                                           ot4[:, (kb % 2) * 512:(kb % 2 + 1) * 512]),
                               f"ost{kb % 2}", f"4st_{oc}_{tt}", 16)
                    blk.scalar(_)

            def _(s):
                tk.wait(s, "4st_31_0", "4st_31_1")
            blk.scalar(_)

        if DEBUG:
            with nc.Block("phdbg") as blk:
                def _(g):
                    tk.inc(g.dma_start(dbg["d_yd"], y_d), "g", "dbgf1", 16)
                    tk.inc(g.dma_start(dbg["d_gd"], g_d), "g", "dbgf2", 16)
                    tk.inc(g.dma_start(dbg["d_gated"], gate_d), "g", "dbgf3", 16)
                    yv = dbg["d_y2b"].rearrange("(c p) t -> c p t", p=128)
                    for oc in range(64):
                        tk.inc(g.dma_start(yv[oc], y2[:, oc * TOK:(oc + 1) * TOK]), "g", "dbgf4", 16)
                    tk.wait(g, "dbgf4")
                blk.gpsimd(_)

    return nc


_NC_CACHE = {}


def _get_nc():
    if "nc" not in _NC_CACHE:
        _NC_CACHE["nc"] = build_nc()
    return _NC_CACHE["nc"]


def kernel(**inputs):
    x = np.ascontiguousarray(np.asarray(inputs["L0_attn_x"], dtype=np.float32))
    Win = np.asarray(inputs["L0_attn_in_proj_w"], dtype=np.float32)
    Wout = np.asarray(inputs["L0_attn_out_proj_w"], dtype=np.float32)
    Wff = np.ascontiguousarray(np.asarray(inputs["L0_ffn_kernel"], dtype=np.float32))
    cw = np.asarray(inputs["L0_attn_conv_w"], dtype=np.float32)
    cb = np.asarray(inputs["L0_attn_conv_b"], dtype=np.float32)
    nw = np.asarray(inputs["L0_attn_norm_w"], dtype=np.float32)
    dtb = np.asarray(inputs["L0_attn_dt_bias"], dtype=np.float32)
    alog = np.asarray(inputs["L0_attn_A_log"], dtype=np.float32)
    dp = np.asarray(inputs["L0_attn_Dp"], dtype=np.float32)
    gnw = np.asarray(inputs["L0_attn_gnorm_w"], dtype=np.float32)
    gate = np.asarray(inputs["L0_ffn_gate"], dtype=np.float32)
    ffb = np.asarray(inputs["L0_ffn_bias"], dtype=np.float32)
    lA = np.asarray(inputs["L0_ffn_lora_A"], dtype=np.float32)
    lB = np.asarray(inputs["L0_ffn_lora_B"], dtype=np.float32)
    lAs = np.asarray(inputs["L0_ffn_lora_A_shared"], dtype=np.float32)
    lBs = np.asarray(inputs["L0_ffn_lora_B_shared"], dtype=np.float32)

    WinT = np.ascontiguousarray(
        Win.T.reshape(32, 128, PROJD // 128, 128).transpose(2, 1, 0, 3).reshape(PROJD // 128, 128, DM))
    WoutT = np.ascontiguousarray(
        Wout.T.reshape(64, 128, 32, 128).transpose(2, 1, 0, 3).reshape(32, 128, DI))
    Wff = np.ascontiguousarray(
        Wff.reshape(32, 128, 32, 128).transpose(2, 1, 0, 3).reshape(32, 128, DM))
    nw_re = np.ascontiguousarray(nw.reshape(NCD, 128).T)
    cw_re = np.ascontiguousarray(cw.reshape(80, 128, 4).transpose(1, 0, 2).reshape(128, 320))
    cb_re = np.ascontiguousarray(cb.reshape(80, 128).T)
    dtb_re = np.ascontiguousarray(dtb.reshape(128, 1))
    alog_re = np.ascontiguousarray(alog.reshape(128, 1))
    dp_re = np.ascontiguousarray(dp.reshape(1, 128))
    gnw_re = np.ascontiguousarray(gnw.reshape(64, 128).T)
    dpoc_re = np.ascontiguousarray(np.repeat(dp.reshape(64, 2), 64, axis=1).reshape(64, 128).T)
    ffnb_re = np.ascontiguousarray(ffb.reshape(NCD, 128).T)
    lA_re = np.ascontiguousarray(
        lA.transpose(0, 2, 1).reshape(16, NCD, 128, 32).transpose(0, 2, 1, 3).reshape(16, 128, NCD * 32))
    lB_re = np.ascontiguousarray(
        lB.reshape(16, 4, 1024, 32).transpose(0, 1, 3, 2).reshape(16, 128, 1024))
    lAs_re = np.ascontiguousarray(
        lAs.T.reshape(NCD, 128, 32).transpose(1, 0, 2).reshape(128, NCD * 32))
    lBs_re = np.ascontiguousarray(
        lBs.reshape(4, 1024, 32).transpose(0, 2, 1).reshape(128, 1024))

    in_maps = []
    for k in range(8):
        b, q = k // 4, k % 4
        xT = np.ascontiguousarray(x[b].T)
        halo = np.zeros((DM, 3), np.float32) if q == 0 else xT[:, 1024 * q - 3:1024 * q]
        xfm_k = np.ascontiguousarray(np.concatenate([halo, xT[:, 1024 * q:1024 * (q + 1)]], axis=1))
        selt = np.zeros((8, 8), np.float32)
        maskv = np.full((8, 1), -1e9, np.float32)
        for m in range(8):
            if m // 4 == b and m < k:
                maskv[m, 0] = 0.0
            for j in range(8):
                if j // 4 == b and m < j < k:
                    selt[j, m] = 1.0
        in_maps.append({
            "xfm": xfm_k, "WinT": WinT, "WoutT": WoutT, "Wffn": Wff,
            "nw_re": nw_re, "cw_re": cw_re, "cb_re": cb_re, "dtb_re": dtb_re,
            "alog_re": alog_re, "dp_re": dp_re, "gnw_re": gnw_re, "dpoc_re": dpoc_re,
            "g16_re": np.ascontiguousarray(gate[k].reshape(1, 16)),
            "selt": selt, "maskv": maskv,
            "lA_re": lA_re, "lB_re": lB_re, "lAs_re": lAs_re, "lBs_re": lBs_re,
            "ffnb_re": ffnb_re,
        })

    nc = _get_nc()
    res = run_bass_kernel_spmd(nc, in_maps, core_ids=list(range(8)))
    out = np.empty((2, 4096, 4096), dtype=np.float32)
    for k in range(8):
        b, q = k // 4, k % 4
        out[b, 1024 * q:1024 * (q + 1), :] = res.results[k]["out_fm"].T
    if DEBUG:
        kernel.last_results = res
    return out


if __name__ == "__main__":
    nc = build_nc()
    print("instructions:", nc.n_instructions())


# version 17 (final)
# speedup vs baseline: 1.0032x; 1.0032x over previous
import contextlib
import numpy as np
import ml_dtypes
import concourse.bass as bass
import concourse.mybir as mybir
from concourse.bass_utils import run_bass_kernel_spmd

F32 = mybir.dt.float32
BF16 = mybir.dt.bfloat16
AF = mybir.ActivationFunctionType
ALU = mybir.AluOpType
BF = ml_dtypes.bfloat16

DEBUG = False
STOP = None

DM, DI, G, N, H, P = 4096, 8192, 8, 128, 128, 64
CONVD = DI + 2 * G * N
PROJD = DI + CONVD + H
TOK, THL = 1024, 1027
NCD = DM // 128
NCH = 8
EPS = 1e-5
TT0 = [3, 515, 0]
TTW = [512, 512, 3]
OC_DT = 144
OC_ORDER = list(range(128, 144)) + list(range(64, 128)) + list(range(0, 64))


class TK:

    def __init__(self, nc):
        self.nc = nc
        self.sems = {}
        self.cnt = {}
        self.tokens = {}
        self.resv = {}

    def inc(self, instr, stream, name, n=1):
        if stream not in self.sems:
            self.sems[stream] = self.nc.alloc_semaphore("tk_" + stream)
            self.cnt[stream] = 0
        instr.then_inc(self.sems[stream], n)
        self.cnt[stream] += n
        self.tokens[name] = (self.sems[stream], self.cnt[stream])
        return instr

    def reserve(self, stream, name, n=1):
        if stream not in self.sems:
            self.sems[stream] = self.nc.alloc_semaphore("tk_" + stream)
            self.cnt[stream] = 0
        self.cnt[stream] += n
        self.tokens[name] = (self.sems[stream], self.cnt[stream])
        self.resv.setdefault(stream, []).append((name, n))

    def bind(self, instr, name):
        for stream, lst in self.resv.items():
            if lst and lst[0][0] == name:
                nm, n = lst.pop(0)
                instr.then_inc(self.sems[stream], n)
                return instr
        raise KeyError(f"bind out of order or unreserved: {name}")

    def tok(self, name):
        return self.tokens[name]

    def wait(self, eng, *names):
        for nm in names:
            s, c = self.tokens[nm]
            eng.wait_ge(s, c)


def build_nc():
    nc = bass.Bass("TRN2", target_bir_lowering=False, detect_race_conditions=False)
    tk = TK(nc)

    DT = nc.dram_tensor
    xfm = DT("xfm", [DM, THL], F32, kind="ExternalInput").ap()
    WinT = DT("WinT", [PROJD // 128, 128, DM], F32, kind="ExternalInput").ap()
    WoutT = DT("WoutT", [32, 128, DI], F32, kind="ExternalInput").ap()
    Wffn = DT("Wffn", [32, 128, DM], F32, kind="ExternalInput").ap()
    nw_in = DT("nw_re", [128, NCD], F32, kind="ExternalInput").ap()
    cw_in = DT("cw_re", [128, 320], F32, kind="ExternalInput").ap()
    cb_in = DT("cb_re", [128, 80], F32, kind="ExternalInput").ap()
    dtb_in = DT("dtb_re", [128, 1], F32, kind="ExternalInput").ap()
    alog_in = DT("alog_re", [128, 1], F32, kind="ExternalInput").ap()
    dp_in = DT("dp_re", [1, 128], F32, kind="ExternalInput").ap()
    gnw_in = DT("gnw_re", [128, 64], F32, kind="ExternalInput").ap()
    dpoc_in = DT("dpoc_re", [128, 64], F32, kind="ExternalInput").ap()
    g16_in = DT("g16_re", [1, 16], F32, kind="ExternalInput").ap()
    selt_in = DT("selt", [8, 8], F32, kind="ExternalInput").ap()
    maskv_in = DT("maskv", [8, 1], F32, kind="ExternalInput").ap()
    lA_in = DT("lA_re", [16, 128, 1024], F32, kind="ExternalInput").ap()
    lB_in = DT("lB_re", [16, 128, 1024], F32, kind="ExternalInput").ap()
    lAs_in = DT("lAs_re", [128, NCD * 32], F32, kind="ExternalInput").ap()
    lBs_in = DT("lBs_re", [128, 1024], F32, kind="ExternalInput").ap()
    ffnb_in = DT("ffnb_re", [128, NCD], F32, kind="ExternalInput").ap()
    out_fm = DT("out_fm", [DM, TOK], F32, kind="ExternalOutput").ap()

    hidf_d = DT("hidf_d", [DI, TOK], BF16).ap()
    hidt_d = DT("hidt_d", [NCH, 128, DI], BF16).ap()
    gate_d = DT("gate_d", [DI, TOK], BF16).ap()
    C_d = DT("C_dd", [G * N, TOK], BF16).ap()
    s_d = DT("s_dd", [NCH, N, H * P], BF16).ap()
    g_d = DT("g_dd", [DM, TOK], BF16).ap()
    y_d = DT("y_dd", [DI, TOK], BF16).ap()
    ccSi = DT("ccSi", [N, H * P], BF16).ap()
    ccSo = DT("ccSo", [8 * N, H * P], BF16, addr_space="Shared").ap()
    ccLi = DT("ccLi", [H, 1], F32).ap()
    ccLo = DT("ccLo", [8 * H, 1], F32, addr_space="Shared").ap()

    dbg = {}
    if DEBUG:
        for nm, shp in [("d_xcn", [DM, THL]), ("d_dt", [128, THL]),
                        ("d_sloc", [N, H * P]), ("d_sin", [N, H * P]),
                        ("d_xn", [DM, TOK]), ("d_mu", [128, THL]),
                        ("d_ct", [128, THL]), ("d_y2", [DI, TOK])]:
            dbg[nm] = DT(nm, shp, F32, kind="ExternalOutput").ap()
        for nm, shp in [("d_yd", [DI, TOK]), ("d_gd", [DM, TOK]),
                        ("d_gated", [DI, TOK])]:
            dbg[nm] = DT(nm, shp, BF16, kind="ExternalOutput").ap()
        dbg["d_y2b"] = DT("d_y2b", [DI, TOK], F32, kind="ExternalOutput").ap()

    ii = np.arange(128)
    Unp = (ii[:, None] <= ii[None, :]).astype(BF)
    c_U = nc.inline_tensor(Unp, "cU")
    c_U2 = nc.inline_tensor(np.tile(Unp, (1, 2)).astype(BF), "cU2")
    c_U4 = nc.inline_tensor(np.tile(Unp, (1, 4)).astype(BF), "cU4")
    c_Us = nc.inline_tensor((ii[:, None] > ii[None, :]).astype(BF), "cUs")
    c_ao = nc.inline_tensor(np.ones((128, 128), dtype=BF), "cAO")
    c_m01 = nc.inline_tensor((ii[None, :] >= ii[:, None]).astype(BF), "cM01")
    c_idb = nc.inline_tensor(np.eye(128, dtype=BF), "cIdb")
    c_idf = nc.inline_tensor(np.eye(128, dtype=np.float32), "cIdf")
    c_of = nc.inline_tensor(np.ones((1, 128), dtype=np.float32), "cOF")
    dm8 = np.zeros((8, 8, 128), dtype=BF)
    for m in range(8):
        dm8[m, m, :] = 1
    c_dm8 = nc.inline_tensor(dm8.reshape(8, 1024), "cDM8")
    c_ao8 = nc.inline_tensor(np.ones((8, 128), dtype=BF), "cAO8")

    sb = nc.alloc_sbuf_tensor
    t_U = sb("tU", [128, 128], BF16)
    t_U2 = sb("tU2", [128, 256], BF16)
    t_U4 = sb("tU4", [128, 512], BF16)
    t_Us = sb("tUs", [128, 128], BF16)
    t_ao = sb("tAO", [128, 128], BF16)
    t_m01 = sb("tM01", [128, 128], BF16)
    t_idb = sb("tIdb", [128, 128], BF16)
    t_idf = sb("tIdf", [128, 128], F32)
    t_of = sb("tOF", [1, 128], F32)
    t_dm8 = sb("tDM8", [8, 1024], BF16)
    t_ao8 = sb("tAO8", [8, 128], BF16)
    t_nw = sb("tNW", [128, NCD], F32)
    t_cw = sb("tCW", [128, 320], F32)
    t_cb = sb("tCB", [128, 80], F32)
    t_dtb = sb("tDTB", [128, 1], F32)
    t_aneg = sb("tANEG", [128, 1], F32)
    t_dprow = sb("tDPROW", [1, 128], F32)
    t_dpb = sb("tDPB", [128, 128], F32)
    t_gnw = sb("tGNW", [128, 64], F32)
    t_dpoc = sb("tDPOC", [128, 64], F32)
    t_g16 = sb("tG16", [1, 16], F32)
    t_gbc = sb("tGBC", [128, 16], F32)
    t_selt = sb("tSELT", [8, 8], F32)
    t_maskv = sb("tMASKV", [8, 1], F32)
    t_ffnb = sb("tFFNB", [128, NCD], F32)
    t_mu = sb("tMU", [128, THL], F32)
    t_c2 = sb("tC2", [128, THL], F32)
    t_Lc = sb("tLC", [128, 1], F32)

    CONSTLOADS = [
        (t_U, c_U.ap()), (t_U2, c_U2.ap()), (t_U4, c_U4.ap()), (t_Us, c_Us.ap()),
        (t_ao, c_ao.ap()), (t_m01, c_m01.ap()), (t_idb, c_idb.ap()),
        (t_idf, c_idf.ap()), (t_of, c_of.ap()), (t_dm8, c_dm8.ap()),
        (t_ao8, c_ao8.ap()), (t_nw, nw_in), (t_cw, cw_in), (t_cb, cb_in),
        (t_dtb, dtb_in), (t_dprow, dp_in), (t_gnw, gnw_in), (t_dpoc, dpoc_in),
        (t_g16, g16_in),
        (t_selt, selt_in), (t_maskv, maskv_in), (t_ffnb, ffnb_in),
    ]

    MID = contextlib.ExitStack()
    with MID:
        t_afT = MID.enter_context(nc.sbuf_tensor("tAFT", [128, NCH * 128], BF16))
        t_dtT = MID.enter_context(nc.sbuf_tensor("tDTT", [128, NCH * 128], F32))
        t_dsT = MID.enter_context(nc.sbuf_tensor("tDST", [128, NCH * 128], F32))
        t_dcb = MID.enter_context(nc.sbuf_tensor("tDCB", [128, NCH * 128], F32))
        t_Btm = MID.enter_context(nc.sbuf_tensor("tBTM", [128, 64 * 128], BF16))
        t_STm = MID.enter_context(nc.sbuf_tensor("tSTM", [128, 64 * 128], BF16))
        XS = contextlib.ExitStack()
        x_cn = XS.enter_context(nc.sbuf_tensor("x_cn", [128, NCD * THL], BF16))
        t_ct = XS.enter_context(nc.sbuf_tensor("t_ct", [128, THL], F32))
        if True:
            ES1 = contextlib.ExitStack()
            with ES1:
                p0x = ES1.enter_context(nc.sbuf_tensor("p0x", [128, 2 * THL], F32))
                p0xb = ES1.enter_context(nc.sbuf_tensor("p0xb", [128, 2 * THL], BF16))
                p0xq = ES1.enter_context(nc.sbuf_tensor("p0xq", [128, 2 * THL], BF16))
                p0s1 = ES1.enter_context(nc.sbuf_tensor("p0s1", [128, THL], F32))
                p0s2 = ES1.enter_context(nc.sbuf_tensor("p0s2", [128, THL], F32))
                p0s3 = ES1.enter_context(nc.sbuf_tensor("p0s3", [128, THL], F32))
                psA0 = ES1.enter_context(nc.psum_tensor("psA0", [128, 512], F32))
                psA1 = ES1.enter_context(nc.psum_tensor("psA1", [128, 512], F32))
                psA2 = ES1.enter_context(nc.psum_tensor("psA2", [128, 512], F32))
                psB0 = ES1.enter_context(nc.psum_tensor("psB0", [128, 512], F32))
                psB1 = ES1.enter_context(nc.psum_tensor("psB1", [128, 512], F32))
                psB2 = ES1.enter_context(nc.psum_tensor("psB2", [128, 512], F32))
                psR = ES1.enter_context(nc.psum_tensor("psR", [128, 128], F32))
                blk = ES1.enter_context(nc.Block("ph0"))
                psA = [psA0, psA1, psA2]
                psB = [psB0, psB1, psB2]

                def _(sy):
                    for dst, src in CONSTLOADS:
                        tk.inc(sy.dma_start(dst[:, :], src), "sy", "c0", 16)
                    tk.inc(sy.dma_start(p0s1[0:128, 0:1], alog_in), "sy", "c0", 16)
                blk.sync(_)

                def _(s):
                    tk.wait(s, "c0")
                    s.activation(t_aneg[:, :], p0s1[0:128, 0:1], AF.Exp)
                    tk.inc(s.activation(t_aneg[:, :], t_aneg[:, :], AF.Copy, scale=-1.0),
                           "s", "aneg")
                blk.scalar(_)

                for d in range(NCD):
                    par = d % 2

                    def _(sy, d=d, par=par):
                        if d == 0:
                            tk.wait(sy, "aneg")
                        if d >= 2:
                            tk.wait(sy, f"p0cast{d - 2}")
                        tk.inc(sy.dma_start(p0x[:, par * THL:(par + 1) * THL],
                                            xfm[128 * d:128 * (d + 1), :]), f"fx{par}", f"p0x{d}", 16)
                    blk.sync(_)

                    def _(v, d=d, par=par):
                        tk.wait(v, f"p0x{d}")
                        if d >= 2:
                            tk.wait(v, f"p0mm{d - 2}")
                        tk.inc(v.tensor_copy(p0xb[:, par * THL:(par + 1) * THL],
                                             p0x[:, par * THL:(par + 1) * THL]),
                               "v", f"p0cast{d}")
                    blk.vector(_)

                    def _(s, d=d, par=par):
                        tk.wait(s, f"p0cast{d}")
                        if d >= 2:
                            tk.wait(s, f"p0mm{d - 2}")
                        tk.inc(s.activation(p0xq[:, par * THL:(par + 1) * THL],
                                            p0xb[:, par * THL:(par + 1) * THL], AF.Square),
                               "s", f"p0sq{d}")
                    blk.scalar(_)

                    def _(t, d=d, par=par):
                        tk.wait(t, f"p0sq{d}")
                        for j in range(3):
                            sl = slice(par * THL + TT0[j], par * THL + TT0[j] + TTW[j])
                            t.matmul(psA[j][:, 0:TTW[j]], t_ao[:, :], p0xb[:, sl],
                                     start=(d == 0), stop=(d == NCD - 1))
                            mm = t.matmul(psB[j][:, 0:TTW[j]], t_ao[:, :], p0xq[:, sl],
                                          start=(d == 0), stop=(d == NCD - 1))
                        tk.inc(mm, "t", f"p0mm{d}")
                    blk.tensor(_)

                def _(t):
                    tk.inc(t.matmul(psR[:, :], t_of[:, :], t_dprow[:, :],
                                    start=True, stop=True), "t", "dpmm")
                blk.tensor(_)

                def _(v):
                    tk.wait(v, f"p0mm{NCD - 1}")
                    for j in range(3):
                        v.tensor_scalar(t_mu[:, TT0[j]:TT0[j] + TTW[j]], psA[j][:, 0:TTW[j]],
                                        1.0 / DM, None, ALU.mult)
                        v.tensor_scalar(p0s2[:, TT0[j]:TT0[j] + TTW[j]], psB[j][:, 0:TTW[j]],
                                        1.0 / DM, None, ALU.mult)
                    v.tensor_mul(p0s3[:, :], t_mu[:, :], t_mu[:, :])
                    v.tensor_sub(p0s2[:, :], p0s2[:, :], p0s3[:, :])
                    v.tensor_scalar(p0s3[:, :], p0s2[:, :], 1.0 + EPS, EPS * EPS, ALU.mult, ALU.add)
                    tk.inc(v.tensor_scalar(p0s2[:, :], p0s2[:, :], 1.0, EPS, ALU.mult, ALU.add),
                           "v", "rsargs")
                    tk.wait(v, "dpmm")
                    tk.inc(v.tensor_copy(t_dpb[:, :], psR[:, :]), "v", "dpbc")
                blk.vector(_)

                def _(s):
                    tk.wait(s, "rsargs")
                    s.activation(t_ct[:, :], p0s3[:, :], AF.Sqrt)
                    tk.inc(s.activation(t_c2[:, :], p0s2[:, :], AF.Sqrt),
                           "s", "rsraw")
                blk.scalar(_)

                def _(v):
                    tk.wait(v, "rsraw")
                    v.reciprocal(t_ct[:, :], t_ct[:, :])
                    v.reciprocal(t_c2[:, :], t_c2[:, :])
                    for arg, y in ((p0s3, t_ct), (p0s2, t_c2)):
                        v.tensor_mul(p0s1[:, :], y[:, :], y[:, :])
                        v.tensor_mul(p0s1[:, :], p0s1[:, :], arg[:, :])
                        v.tensor_scalar(p0s1[:, :], p0s1[:, :], -0.5, 1.5, ALU.mult, ALU.add)
                        v.tensor_mul(y[:, :], y[:, :], p0s1[:, :])
                    tk.inc(v.engine_nop(), "v", "ctdone")
                blk.vector(_)

                for d in range(NCD):
                    par = d % 2

                    def _(sy, d=d, par=par):
                        if d == 0:
                            tk.wait(sy, "ctdone")
                        if d >= 2:
                            tk.wait(sy, f"xcn{d - 2}")
                        tk.inc(sy.dma_start(p0x[:, par * THL:(par + 1) * THL],
                                            xfm[128 * d:128 * (d + 1), :]), f"fx{par}", f"p0y{d}", 16)
                    blk.sync(_)

                    def _(v, d=d, par=par):
                        tk.wait(v, f"p0y{d}")
                        v.tensor_sub(p0s1[:, :], p0x[:, par * THL:(par + 1) * THL], t_mu[:, :])
                        tk.inc(v.tensor_scalar(x_cn[:, d * THL:(d + 1) * THL], p0s1[:, :],
                                               t_nw[:, d:d + 1], None, ALU.mult), "v", f"xcn{d}")
                    blk.vector(_)

                if DEBUG:
                    def _(g):
                        tk.wait(g, f"xcn{NCD - 1}")
                        xv = dbg["d_xcn"].rearrange("(c p) t -> c p t", p=128)
                        for d in range(NCD):
                            tk.inc(g.dma_start(xv[d], x_cn[:, d * THL:(d + 1) * THL]), "g", "dbg1", 16)
                        tk.inc(g.dma_start(dbg["d_mu"], t_mu[:, :]), "g", "dbg2", 16)
                        tk.inc(g.dma_start(dbg["d_ct"], t_ct[:, :]), "g", "dbg3", 16)
                    blk.gpsimd(_)

            if STOP == "p0":
                XS.close()
                return nc
            ES2 = contextlib.ExitStack()
            with ES2:
                wdt = ES2.enter_context(nc.sbuf_tensor("wdt", [128, NCD * 128], BF16))
                pj0 = ES2.enter_context(nc.sbuf_tensor("pj0", [128, THL], F32))
                dtsb = ES2.enter_context(nc.sbuf_tensor("dtsb", [128, THL], F32))
                afsb = ES2.enter_context(nc.sbuf_tensor("afsb", [128, THL], F32))
                acssb = ES2.enter_context(nc.sbuf_tensor("acssb", [128, 128], F32))
                dssb = ES2.enter_context(nc.sbuf_tensor("dssb", [128, 128], F32))
                lastc = ES2.enter_context(nc.sbuf_tensor("lastc", [128, 1], F32))
                pjA = ES2.enter_context(nc.psum_tensor("pjA", [128, 512], F32))
                pjB = ES2.enter_context(nc.psum_tensor("pjB", [128, 512], F32))
                pjC = ES2.enter_context(nc.psum_tensor("pjC", [128, 512], F32))
                pt0 = ES2.enter_context(nc.psum_tensor("pt0", [128, 128], F32))
                pt1 = ES2.enter_context(nc.psum_tensor("pt1", [128, 128], F32))
                blk = ES2.enter_context(nc.Block("ph1dt"))
                pj = [pjA, pjB, pjC]

                def _(g):
                    tk.inc(g.dma_start(wdt[:, :], WinT[OC_DT]), "wdt", "wdt", 16)
                blk.gpsimd(_)

                def _(t):
                    tk.wait(t, "wdt")
                    for d in range(NCD):
                        for j in range(3):
                            mm = t.matmul(pj[j][:, 0:TTW[j]], wdt[:, d * 128:(d + 1) * 128],
                                          x_cn[:, d * THL + TT0[j]:d * THL + TT0[j] + TTW[j]],
                                          start=(d == 0), stop=(d == NCD - 1))
                    tk.inc(mm, "t", "dtmm")
                blk.tensor(_)

                def _(v):
                    tk.wait(v, "dtmm")
                    for j in range(3):
                        v.tensor_mul(pj0[:, TT0[j]:TT0[j] + TTW[j]], pj[j][:, 0:TTW[j]],
                                     t_ct[:, TT0[j]:TT0[j] + TTW[j]])
                    tk.inc(v.engine_nop(), "v", "dtevac")
                blk.vector(_)

                def _(s):
                    tk.wait(s, "dtevac")
                    s.activation(dtsb[:, :], pj0[:, :], AF.Abs, bias=t_dtb[:, :])
                    s.activation(dtsb[:, :], dtsb[:, :], AF.Exp, scale=-1.0)
                    s.activation(dtsb[:, :], dtsb[:, :], AF.Ln, bias=1.0)
                    tk.inc(s.activation(afsb[:, :], pj0[:, :], AF.Relu, bias=t_dtb[:, :]),
                           "s", "sfp")
                blk.scalar(_)

                def _(v):
                    tk.wait(v, "sfp")
                    v.tensor_add(dtsb[:, :], afsb[:, :], dtsb[:, :])
                    v.memset(t_Lc[:, :], 0.0)
                    tk.inc(v.tensor_scalar(afsb[:, :], dtsb[:, :], t_aneg[:, :], None, ALU.mult),
                           "v", "afull")
                blk.vector(_)

                for c in range(NCH):
                    sl = slice(3 + 128 * c, 3 + 128 * (c + 1))
                    csl = slice(128 * c, 128 * (c + 1))

                    def _(t, c=c, sl=sl):
                        tk.wait(t, "afull")
                        if c >= 1:
                            tk.wait(t, f"adsT_{c - 1}")
                        t.transpose(pt0[:, :], dtsb[:, sl], t_idf[:, :])
                        tk.inc(t.transpose(pt1[:, :], afsb[:, sl], t_idf[:, :]), "t", f"atr_{c}")
                    blk.tensor(_)

                    def _(v, c=c, csl=csl):
                        tk.wait(v, f"atr_{c}")
                        v.tensor_copy(t_dtT[:, csl], pt0[:, :])
                        tk.inc(v.tensor_copy(t_afT[:, csl], pt1[:, :]), "v", f"aev_{c}")
                    blk.vector(_)

                    def _(t, c=c, csl=csl):
                        tk.wait(t, f"aev_{c}")
                        t.matmul(pt0[:, :], t_afT[:, csl], t_U[:, :], start=True, stop=True)
                        tk.inc(t.matmul(pt1[:, :], t_ao[:, :], t_afT[:, csl],
                                        start=True, stop=True), "t", f"amm_{c}")
                    blk.tensor(_)

                    def _(v, c=c):
                        tk.wait(v, f"amm_{c}")
                        v.tensor_copy(acssb[:, :], pt0[:, :])
                        v.tensor_copy(lastc[:, :], pt0[:, 127:128])
                        tk.inc(v.tensor_add(t_Lc[:, :], t_Lc[:, :], lastc[:, :]), "v", f"aacs_{c}")
                    blk.vector(_)

                    def _(s, c=c, csl=csl):
                        tk.wait(s, f"aacs_{c}")
                        s.activation(dssb[:, :], acssb[:, :], AF.Exp, scale=-1.0, bias=lastc[:, :])
                        tk.inc(s.activation(t_dcb[:, csl], pt1[:, :], AF.Exp), "s", f"aexp_{c}")
                    blk.scalar(_)

                    def _(t, c=c):
                        tk.wait(t, f"aexp_{c}")
                        tk.inc(t.transpose(pt0[:, :], dssb[:, :], t_idf[:, :]), "t", f"atr2_{c}")
                    blk.tensor(_)

                    def _(v, c=c, csl=csl):
                        tk.wait(v, f"atr2_{c}")
                        tk.inc(v.tensor_copy(t_dsT[:, csl], pt0[:, :]), "v", f"adsT_{c}")
                    blk.vector(_)

                if DEBUG:
                    def _(g):
                        tk.wait(g, f"adsT_{NCH - 1}")
                        tk.inc(g.dma_start(dbg["d_dt"], dtsb[:, :]), "g", "dbg4", 16)
                    blk.gpsimd(_)

            if STOP == "p1dt":
                XS.close()
                return nc
            ES3 = contextlib.ExitStack()
            with ES3:
                wsl = ES3.enter_context(nc.sbuf_tensor("wsl", [128, 2 * NCD * 128], BF16))
                pjs = ES3.enter_context(nc.sbuf_tensor("pjs", [128, 2 * THL], F32))
                cacc = ES3.enter_context(nc.sbuf_tensor("cacc", [128, 2 * TOK], F32))
                Bfm = ES3.enter_context(nc.sbuf_tensor("Bfm", [128, 8 * TOK], BF16))
                stg = ES3.enter_context(nc.sbuf_tensor("stg", [128, 2 * TOK], BF16))
                stgT = ES3.enter_context(nc.sbuf_tensor("stgT", [128, 2 * TOK], BF16))
                qjA0 = ES3.enter_context(nc.psum_tensor("qjA0", [128, 512], F32))
                qjB0 = ES3.enter_context(nc.psum_tensor("qjB0", [128, 512], F32))
                qjC0 = ES3.enter_context(nc.psum_tensor("qjC0", [128, 512], F32))
                qjA1 = ES3.enter_context(nc.psum_tensor("qjA1", [128, 512], F32))
                qjB1 = ES3.enter_context(nc.psum_tensor("qjB1", [128, 512], F32))
                qjC1 = ES3.enter_context(nc.psum_tensor("qjC1", [128, 512], F32))
                qtb0 = ES3.enter_context(nc.psum_tensor("qtb0", [128, 128], F32))
                qtb1 = ES3.enter_context(nc.psum_tensor("qtb1", [128, 128], F32))
                blk = ES3.enter_context(nc.Block("ph1m"))
                qj2 = [[qjA0, qjB0, qjC0], [qjA1, qjB1, qjC1]]
                qtb = [qtb0, qtb1]
                naux = [0]
                aux_last = {}
                NSEQ = len(OC_ORDER)

                def octype(oc):
                    if oc >= 136:
                        return "C"
                    if oc >= 128:
                        return "B"
                    if oc >= 64:
                        return "hid"
                    return "gate"

                def emit_slab(i):
                    oc = OC_ORDER[i]

                    def _(g, i=i, oc=oc):
                        if i >= 2:
                            tk.wait(g, f"mm_{i - 2}")
                        dst = wsl[:, (i % 2) * NCD * 128:(i % 2 + 1) * NCD * 128]
                        tk.inc(g.dma_start(dst, WinT[oc]), f"wsl{i % 2}", f"wsl_{i}", 16)
                    blk.gpsimd(_)

                def emit_mm(i):
                    def _(t, i=i):
                        tk.wait(t, f"wsl_{i}")
                        if i >= 2:
                            tk.wait(t, f"evac_{i - 2}")
                            if octype(OC_ORDER[i - 2]) == "C" and (i - 2) in aux_last:
                                tk.wait(t, f"vax_{aux_last[i - 2]}")
                        base = (i % 2) * NCD * 128
                        qj = qj2[i % 2]
                        nj = 3 if OC_ORDER[i] >= 64 else 2
                        for d in range(NCD):
                            for j in range(nj):
                                mm = t.matmul(qj[j][:, 0:TTW[j]],
                                              wsl[:, base + d * 128:base + (d + 1) * 128],
                                              x_cn[:, d * THL + TT0[j]:d * THL + TT0[j] + TTW[j]],
                                              start=(d == 0), stop=(d == NCD - 1))
                        tk.inc(mm, "t", f"mm_{i}")
                    blk.tensor(_)

                def emit_evac_conv(i):
                    oc = OC_ORDER[i]
                    ty = octype(oc)

                    def _(v, i=i, oc=oc, ty=ty):
                        pp = i % 2
                        tk.wait(v, f"mm_{i}")
                        if i >= 2:
                            tk.wait(v, f"sact_{i - 2}")
                        nj = 3 if oc >= 64 else 2
                        for j in range(nj):
                            v.tensor_mul(pjs[:, pp * THL + TT0[j]:pp * THL + TT0[j] + TTW[j]],
                                         qj2[i % 2][j][:, 0:TTW[j]], t_ct[:, TT0[j]:TT0[j] + TTW[j]])
                        tk.inc(v.engine_nop(), "v", f"evac_{i}")
                        if ty != "gate":
                            hc = oc - 64
                            base = pjs[:, pp * THL:(pp + 1) * THL]
                            ca = cacc[:, pp * TOK:(pp + 1) * TOK]
                            v.tensor_scalar(ca, base[:, 0:TOK], t_cw[:, 4 * hc:4 * hc + 1],
                                            None, ALU.mult)
                            for kk in range(1, 4):
                                v.scalar_tensor_tensor(ca, base[:, kk:kk + TOK],
                                                       t_cw[:, 4 * hc + kk:4 * hc + kk + 1],
                                                       ca, ALU.mult, ALU.add)
                        tk.inc(v.engine_nop(), "v", f"conv_{i}")
                    blk.vector(_)

                def emit_silu(i):
                    oc = OC_ORDER[i]
                    ty = octype(oc)

                    def _(s, i=i, oc=oc, ty=ty):
                        pp = i % 2
                        tk.wait(s, f"conv_{i}")
                        if i >= 2:
                            tk.wait(s, f"stgfree_{i - 2}")
                        if ty == "gate":
                            act = s.activation(stg[:, pp * TOK:(pp + 1) * TOK],
                                               pjs[:, pp * THL + 3:pp * THL + 3 + TOK], AF.Silu)
                        elif ty == "B":
                            bg = oc - 128
                            act = s.activation(Bfm[:, bg * TOK:(bg + 1) * TOK],
                                               cacc[:, pp * TOK:(pp + 1) * TOK], AF.Silu,
                                               bias=t_cb[:, (oc - 64):(oc - 64) + 1])
                        else:
                            act = s.activation(stg[:, pp * TOK:(pp + 1) * TOK],
                                               cacc[:, pp * TOK:(pp + 1) * TOK], AF.Silu,
                                               bias=t_cb[:, (oc - 64):(oc - 64) + 1])
                        tk.inc(act, "s", f"sact_{i}")
                    blk.scalar(_)

                def emit_aux(i):
                    oc = OC_ORDER[i]
                    ty = octype(oc)
                    if ty == "gate":
                        return
                    if ty == "C":
                        for c in range(NCH):
                            k = naux[0] + c

                            def _(t, i=i, oc=oc, c=c, k=k):
                                pp = i % 2
                                if c == 0:
                                    tk.wait(t, f"sact_{i}")
                                if k >= 2:
                                    tk.wait(t, f"vax_{k - 2}")
                                gg = oc - 136
                                mm = t.matmul(qtb[k % 2][:, :],
                                              Bfm[:, gg * TOK + 128 * c:gg * TOK + 128 * (c + 1)],
                                              stg[:, pp * TOK + 128 * c:pp * TOK + 128 * (c + 1)],
                                              start=True, stop=True)
                                tk.inc(mm, "t", f"tax_{k}")
                            blk.tensor(_)

                            def _(v, i=i, oc=oc, c=c, k=k):
                                tk.wait(v, f"tax_{k}")
                                gg = oc - 136
                                tk.inc(v.tensor_mul(t_STm[:, (gg * 8 + c) * 128:(gg * 8 + c + 1) * 128],
                                                    qtb[k % 2][:, :], t_m01[:, :]), "v", f"vax_{k}")
                            blk.vector(_)
                        naux[0] += NCH
                        aux_last[i] = naux[0] - 1
                    else:
                        def _(sy, i=i, oc=oc, ty=ty):
                            pp = i % 2
                            tk.wait(sy, f"sact_{i}")
                            for c in range(NCH):
                                if ty == "B":
                                    bg = oc - 128
                                    d = sy.dma_start_transpose(
                                        t_Btm[:, (bg * 8 + c) * 128:(bg * 8 + c + 1) * 128],
                                        Bfm[:, bg * TOK + 128 * c:bg * TOK + 128 * (c + 1)])
                                else:
                                    d = sy.dma_start_transpose(
                                        stgT[:, pp * TOK + 128 * c:pp * TOK + 128 * (c + 1)],
                                        stg[:, pp * TOK + 128 * c:pp * TOK + 128 * (c + 1)])
                                tk.inc(d, "trp", f"trp_{i}_{c}", 16)
                        blk.sync(_)

                def emit_store(i):
                    oc = OC_ORDER[i]
                    ty = octype(oc)
                    pp = i % 2

                    def _(sy, i=i, oc=oc, ty=ty, pp=pp):
                        if ty == "C":
                            gg = oc - 136
                            tk.wait(sy, f"sact_{i}")
                            tk.inc(sy.dma_start(C_d[128 * gg:128 * (gg + 1), :],
                                                stg[:, pp * TOK:(pp + 1) * TOK]), f"stg{pp}", f"st_{i}", 16)
                        elif ty == "gate":
                            tk.wait(sy, f"sact_{i}")
                            tk.inc(sy.dma_start(gate_d[128 * oc:128 * (oc + 1), :],
                                                stg[:, pp * TOK:(pp + 1) * TOK]), f"stg{pp}", f"st_{i}", 16)
                        elif ty == "hid":
                            ohc = oc - 64
                            tk.wait(sy, f"sact_{i}")
                            tk.inc(sy.dma_start(hidf_d[128 * ohc:128 * (ohc + 1), :],
                                                stg[:, pp * TOK:(pp + 1) * TOK]), f"stg{pp}", f"st_{i}a", 16)
                            tk.wait(sy, f"trp_{i}_{NCH - 1}")
                            dstT = hidt_d.rearrange("c p f -> p c f")[:, :, 128 * ohc:128 * (ohc + 1)]
                            tk.inc(sy.dma_start(dstT, stgT[:, pp * TOK:(pp + 1) * TOK]
                                                .rearrange("p (c f) -> p c f", f=128)),
                                    f"stgT{pp}", f"st_{i}", 16)
                        else:
                            tk.inc(sy.nop(), f"stg{pp}", f"st_{i}")
                    blk.sync(_)

                    def _(v, i=i, ty=ty):
                        if ty == "B":
                            tk.wait(v, f"trp_{i}_{NCH - 1}")
                        else:
                            tk.wait(v, f"st_{i}")
                            if ty == "hid":
                                tk.wait(v, f"st_{i}a")
                        tk.inc(v.engine_nop(), "v", f"stgfree_{i}")
                    blk.vector(_)

                for i in range(NSEQ):
                    emit_slab(i)
                    emit_mm(i)
                    if i >= 1:
                        emit_aux(i - 1)
                    emit_evac_conv(i)
                    emit_silu(i)
                    if i >= 1:
                        emit_store(i - 1)
                emit_aux(NSEQ - 1)
                emit_store(NSEQ - 1)

                def _(g):
                    tk.wait(g, "st_78a", "st_79a", "st_78", "st_79",
                            "st_142", "st_143", f"trp_{79}_{NCH - 1}")
                    tk.inc(g.engine_nop(), "gend", "p1st")
                blk.gpsimd(_)

        if STOP == "p1m":
            XS.close()
            return nc
        XS.close()
        ES4 = contextlib.ExitStack()
        with ES4:
            htc = ES4.enter_context(nc.sbuf_tensor("htc", [128, 2 * DI], BF16))
            hdt = ES4.enter_context(nc.sbuf_tensor("hdt", [128, 2 * 128], BF16))
            dtds = ES4.enter_context(nc.sbuf_tensor("dtds", [128, 128], F32))
            scst = ES4.enter_context(nc.sbuf_tensor("scst", [128, H * P], BF16))
            sloc = ES4.enter_context(nc.sbuf_tensor("sloc", [128, H * P], F32))
            psS0 = ES4.enter_context(nc.psum_tensor("psS0", [128, 128], F32))
            psS1 = ES4.enter_context(nc.psum_tensor("psS1", [128, 128], F32))
            blk = ES4.enter_context(nc.Block("ph2a"))
            psS = [psS0, psS1]

            def _(v):
                v.memset(sloc[:, :], 0.0)
                tk.inc(v.engine_nop(), "v", "slocinit")
            blk.vector(_)

            for c in range(NCH):
                par = c % 2

                def _(sy, c=c, par=par):
                    if c == 0:
                        tk.wait(sy, "p1st")
                    if c >= 2:
                        tk.wait(sy, f"2ats_{c - 2}_{63}")
                    tk.inc(sy.dma_start(htc[:, par * DI:(par + 1) * DI], hidt_d[c]),
                           f"htc{par}", f"htc_{c}", 16)
                blk.sync(_)

                def emit_smm(t, c, pr):
                    bpar = pr % 2
                    h0 = 2 * pr
                    k = c * 64 + pr
                    tk.wait(t, f"2ats_{c}_{pr}")
                    if k >= 2:
                        tk.wait(t, f"2aev_{(k - 2) // 64}_{(k - 2) % 64}")
                    gg = h0 // 16
                    for hp in range(2):
                        mm = t.matmul(psS[pr % 2][:, 64 * hp:64 * (hp + 1)],
                                      t_Btm[:, (gg * 8 + c) * 128:(gg * 8 + c + 1) * 128],
                                      hdt[:, bpar * 128 + 64 * hp:bpar * 128 + 64 * (hp + 1)],
                                      start=True, stop=True)
                    tk.bind(mm, f"2amm_{c}_{pr}")

                def emit_sev(v, c, pr):
                    tk.wait(v, f"2amm_{c}_{pr}")
                    if c >= 1 and pr == 0:
                        tk.wait(v, f"sst_{c - 1}")
                    tk.bind(v.tensor_copy(scst[:, 128 * pr:128 * (pr + 1)], psS[pr % 2][:, :]),
                            f"2aev_{c}_{pr}")

                def emit_sts(v, c, pr, par):
                    bpar = pr % 2
                    h0 = 2 * pr
                    k = c * 64 + pr
                    tk.wait(v, f"htc_{c}")
                    if k >= 2:
                        tk.wait(v, f"2amm_{(k - 2) // 64}_{(k - 2) % 64}")
                    op = v.tensor_tensor(hdt[:, bpar * 128:(bpar + 1) * 128].rearrange("p (a f) -> p a f", f=64),
                                         htc[:, par * DI + 128 * pr:par * DI + 128 * (pr + 1)]
                                         .rearrange("p (a f) -> p a f", f=64),
                                         dtds[:, h0:h0 + 2].unsqueeze(2).broadcast_to([128, 2, 64]),
                                         ALU.mult)
                    tk.bind(op, f"2ats_{c}_{pr}")

                tk.reserve("v", f"dtds_{c}")
                for i in range(65):
                    if i < 64:
                        tk.reserve("v", f"2ats_{c}_{i}")
                    if i >= 1:
                        tk.reserve("v", f"2aev_{c}_{i - 1}")
                for i in range(1, 65):
                    tk.reserve("t", f"2amm_{c}_{i - 1}")

                def _(v, c=c, par=par):
                    if c >= 1:
                        tk.wait(v, f"2amm_{c - 1}_{63}")
                    tk.bind(v.tensor_mul(dtds[:, :], t_dtT[:, 128 * c:128 * (c + 1)],
                                         t_dsT[:, 128 * c:128 * (c + 1)]), f"dtds_{c}")
                    for i in range(65):
                        if i < 64:
                            emit_sts(v, c, i, par)
                        if i >= 1:
                            emit_sev(v, c, i - 1)
                blk.vector(_)

                def _(t, c=c):
                    for i in range(1, 65):
                        emit_smm(t, c, i - 1)
                blk.tensor(_)

                def _(v, c=c):
                    tk.wait(v, f"2aev_{c}_{63}", "slocinit")
                    dv = t_dcb[:, 128 * c:128 * (c + 1)].unsqueeze(2).broadcast_to([128, 128, 64])
                    v.tensor_tensor(sloc[:, :].rearrange("p (h q) -> p h q", q=64),
                                    sloc[:, :].rearrange("p (h q) -> p h q", q=64), dv, ALU.mult)
                    tk.inc(v.tensor_add(sloc[:, :], sloc[:, :], scst[:, :]), "v", f"scan_{c}")
                blk.vector(_)

                def _(g, c=c):
                    tk.wait(g, f"scan_{c}")
                    tk.inc(g.dma_start(s_d[c], scst[:, :]), "sst", f"sst_{c}", 16)
                blk.gpsimd(_)

            def _(v):
                tk.wait(v, f"sst_{NCH - 1}")
                tk.inc(v.tensor_copy(scst[:, :], sloc[:, :]), "v", "slocbf")
            blk.vector(_)

            def _(g):
                tk.wait(g, "slocbf")
                tk.inc(g.dma_start(ccSi, scst[:, :]), "ccst", "ccsta", 16)
                tk.inc(g.dma_start(ccLi, t_Lc[:, :]), "ccst", "ccstb", 16)
                if DEBUG:
                    tk.inc(g.dma_start(dbg["d_sloc"], sloc[:, :]), "g", "dbg5", 16)
            blk.gpsimd(_)

        if STOP == "p2a":
            XS.close()
            return nc
        ES5 = contextlib.ExitStack()
        with ES5:
            xn = ES5.enter_context(nc.sbuf_tensor("xn", [128, NCD * TOK], BF16))
            fx = ES5.enter_context(nc.sbuf_tensor("fx", [128, THL], F32))
            fw = ES5.enter_context(nc.sbuf_tensor("fw", [128, 2 * NCD * 128], BF16))
            fwf = ES5.enter_context(nc.sbuf_tensor("fwf", [128, NCD * 128], F32))
            lasf = ES5.enter_context(nc.sbuf_tensor("lasf", [128, NCD * 32], F32))
            lbuf = ES5.enter_context(nc.sbuf_tensor("lbuf", [128, 1024], F32))
            laf = ES5.enter_context(nc.sbuf_tensor("laf", [128, 1024], F32))
            lbf = ES5.enter_context(nc.sbuf_tensor("lbf", [128, 1024], F32))
            labf = ES5.enter_context(nc.sbuf_tensor("labf", [128, 1024], BF16))
            lbbf = ES5.enter_context(nc.sbuf_tensor("lbbf", [32, 4096], BF16))
            lasbf = ES5.enter_context(nc.sbuf_tensor("lasbf", [128, NCD * 32], BF16))
            lbsbf = ES5.enter_context(nc.sbuf_tensor("lbsbf", [32, 4096], BF16))
            axb = ES5.enter_context(nc.sbuf_tensor("axb", [32, 2 * 1024], BF16))
            gst = ES5.enter_context(nc.sbuf_tensor("gst", [128, 2 * 1024], BF16))
            fps0 = ES5.enter_context(nc.psum_tensor("fps0", [128, 512], F32))
            fps1 = ES5.enter_context(nc.psum_tensor("fps1", [128, 512], F32))
            fpa0 = ES5.enter_context(nc.psum_tensor("fpa0", [32, 512], F32))
            fpa1 = ES5.enter_context(nc.psum_tensor("fpa1", [32, 512], F32))
            fpg = ES5.enter_context(nc.psum_tensor("fpg", [128, 16], F32))
            blk = ES5.enter_context(nc.Block("ffn"))
            fps = [fps0, fps1]
            fpa = [fpa0, fpa1]

            def _(g):
                tk.wait(g, "ccsta", "ccstb")
                tk.inc(g.collective_compute("AllGather", ALU.bypass,
                                            replica_groups=[list(range(8))],
                                            ins=[ccLi.opt()], outs=[ccLo.opt()]), "cc", "ccL")
                tk.inc(g.collective_compute("AllGather", ALU.bypass,
                                            replica_groups=[list(range(8))],
                                            ins=[ccSi.opt()], outs=[ccSo.opt()]), "cc", "ccS")
            blk.gpsimd(_)

            def _(t):
                tk.inc(t.matmul(fpg[:, :], t_of[:, :], t_g16[:, :], start=True, stop=True),
                       "t", "gmm")
            blk.tensor(_)

            def _(v):
                tk.wait(v, "gmm")
                tk.inc(v.tensor_copy(t_gbc[:, :], fpg[:, :]), "v", "gbc")
            blk.vector(_)

            for d in range(NCD):
                par = d % 2

                def _(sy, d=d, par=par):
                    if d >= 1:
                        tk.wait(sy, f"fxn_{d - 1}")
                    tk.inc(sy.dma_start(fx[:, :], xfm[128 * d:128 * (d + 1), :]),
                           "fx0", f"fx_{d}", 16)
                blk.sync(_)

                def _(v, d=d, par=par):
                    tk.wait(v, f"fx_{d}")
                    v.tensor_sub(fx[:, 3:3 + TOK], fx[:, 3:3 + TOK], t_mu[:, 3:3 + TOK])
                    tk.inc(v.tensor_mul(xn[:, d * TOK:(d + 1) * TOK], fx[:, 3:3 + TOK],
                                        t_c2[:, 3:3 + TOK]), "v", f"fxn_{d}")
                blk.vector(_)

            if STOP == "ffn_xn":
                return nc
            for which, acc in ((0, laf), (1, lbf)):
                src_d = lA_in if which == 0 else lB_in
                for n in range(16):
                    par = n % 2

                    def _(sy, which=which, n=n, src_d=src_d):
                        if n >= 1:
                            tk.wait(sy, f"mac{which}_{n - 1}")
                        elif which == 1:
                            tk.wait(sy, "mac0_15")
                        tk.inc(sy.dma_start(lbuf[:, :], src_d[n]),
                               "lbuf0", f"lbuf{which}_{n}", 16)
                    blk.sync(_)

                    def _(v, which=which, n=n, acc=acc):
                        tk.wait(v, f"lbuf{which}_{n}")
                        if which == 0 and n == 0:
                            tk.wait(v, "gbc")
                        if n == 0:
                            op = v.tensor_scalar(acc[:, :], lbuf[:, :],
                                                 t_gbc[:, n:n + 1], None, ALU.mult)
                        else:
                            op = v.scalar_tensor_tensor(acc[:, :], lbuf[:, :],
                                                        t_gbc[:, n:n + 1], acc[:, :], ALU.mult, ALU.add)
                        tk.inc(op, "v", f"mac{which}_{n}")
                    blk.vector(_)

            if STOP == "ffn_mac0":
                return nc
            def _(v):
                tk.wait(v, "mac0_15")
                tk.inc(v.tensor_copy(labf[:, :], laf[:, :]), "v", "labf")
            blk.vector(_)
            if STOP == "ffn_mac":
                return nc

            def _(sy):
                tk.inc(sy.dma_start(lasf[:, :], lAs_in), "lashf", "lashf", 16)
            blk.sync(_)

            def _(v):
                tk.wait(v, "lashf")
                tk.inc(v.tensor_copy(lasbf[:, :], lasf[:, :]), "v", "lash")
            blk.vector(_)

            def _(g):
                tk.wait(g, "mac1_15")
                for j in range(4):
                    tk.inc(g.dma_start(lbbf[:, 1024 * j:1024 * (j + 1)],
                                       lbf[32 * j:32 * (j + 1), :]), "lbsh", "lbsh", 16)
                for j in range(4):
                    tk.inc(g.dma_start(lbsbf[:, 1024 * j:1024 * (j + 1)],
                                       lBs_in[32 * j:32 * (j + 1), :]), "lbsh", "lbsh", 16)
            blk.gpsimd(_)

            if STOP == "ffn_sh":
                return nc
            AXSTOP = 1 if STOP == "ffn_ax0" else 2
            for tt in range(AXSTOP):
                def _(t, tt=tt):
                    tk.wait(t, "labf", "lash", f"fxn_{NCD - 1}")
                    if tt >= 1:
                        tk.wait(t, f"axev_{tt - 1}")
                    for d in range(NCD):
                        t.matmul(fpa[0][:, :], labf[:, 32 * d:32 * (d + 1)],
                                 xn[:, d * TOK + 512 * tt:d * TOK + 512 * (tt + 1)],
                                 start=(d == 0), stop=(d == NCD - 1))
                    for d in range(NCD):
                        mm = t.matmul(fpa[1][:, :], lasbf[:, 32 * d:32 * (d + 1)],
                                      xn[:, d * TOK + 512 * tt:d * TOK + 512 * (tt + 1)],
                                      start=(d == 0), stop=(d == NCD - 1))
                    tk.inc(mm, "t", f"axmm_{tt}")
                blk.tensor(_)

                def _(v, tt=tt):
                    tk.wait(v, f"axmm_{tt}")
                    v.tensor_copy(axb[0:32, tt * 1024:tt * 1024 + 512], fpa[0][:, :])
                    tk.inc(v.tensor_copy(axb[0:32, tt * 1024 + 512:(tt + 1) * 1024], fpa[1][:, :]),
                           "v", f"axev_{tt}")
                blk.vector(_)
            if STOP == "ffn_ax0":
                return nc

            if STOP == "ffn_ax":
                return nc
            for i in range(32):
                def _(sy, i=i):
                    if i >= 1:
                        tk.wait(sy, f"fwc_{i - 1}")
                    tk.inc(sy.dma_start(fwf[:, :], Wffn[i]), "fwf", f"fwf_{i}", 16)
                blk.sync(_)

                def _(v, i=i):
                    tk.wait(v, f"fwf_{i}")
                    if i >= 2:
                        tk.wait(v, f"fmm_{i - 2}_1")
                    base = (i % 2) * NCD * 128
                    tk.inc(v.tensor_copy(fw[:, base:base + NCD * 128], fwf[:, :]),
                           "v", f"fwc_{i}")
                blk.vector(_)

                for tt in range(2):
                    def _(t, i=i, tt=tt):
                        tk.wait(t, f"fwc_{i}", f"axev_{tt}", "lbsh")
                        if i >= 1:
                            tk.wait(t, f"fsact_{i - 1}_{tt}")
                        base = (i % 2) * NCD * 128
                        for d in range(NCD):
                            t.matmul(fps[tt][:, :], fw[:, base + 128 * d:base + 128 * (d + 1)],
                                     xn[:, d * TOK + 512 * tt:d * TOK + 512 * (tt + 1)],
                                     start=(d == 0), stop=False)
                        off = 1024 * (i // 8) + 128 * (i % 8)
                        t.matmul(fps[tt][:, :], lbbf[0:32, off:off + 128],
                                 axb[0:32, tt * 1024:tt * 1024 + 512], start=False, stop=False)
                        mm = t.matmul(fps[tt][:, :], lbsbf[0:32, off:off + 128],
                                      axb[0:32, tt * 1024 + 512:(tt + 1) * 1024],
                                      start=False, stop=True)
                        tk.inc(mm, "t", f"fmm_{i}_{tt}")
                    blk.tensor(_)

                    def _(s, i=i, tt=tt):
                        tk.wait(s, f"fmm_{i}_{tt}")
                        if i >= 2:
                            tk.wait(s, f"gstd_{i - 2}")
                        tk.inc(s.activation(gst[:, (i % 2) * 1024 + 512 * tt:(i % 2) * 1024 + 512 * (tt + 1)],
                                            fps[tt][:, :], AF.Silu, bias=t_ffnb[:, i:i + 1]),
                               "s", f"fsact_{i}_{tt}")
                    blk.scalar(_)

                def _(g, i=i):
                    tk.wait(g, f"fsact_{i}_1")
                    tk.inc(g.dma_start(g_d[128 * i:128 * (i + 1), :],
                                       gst[:, (i % 2) * 1024:(i % 2 + 1) * 1024]), f"gstd{i % 2}", f"gstd_{i}", 16)
                blk.gpsimd(_)

            if DEBUG:
                def _(g):
                    tk.wait(g, f"fxn_{NCD - 1}")
                    xv = dbg["d_xn"].rearrange("(c p) t -> c p t", p=128)
                    for d in range(NCD):
                        tk.inc(g.dma_start(xv[d], xn[:, d * TOK:(d + 1) * TOK]), "g", "dbg6", 16)
                    tk.wait(g, "dbg6")
                blk.gpsimd(_)

        if STOP == "ffn":
            XS.close()
            return nc
        ES6 = contextlib.ExitStack()
        with ES6:
            prev = ES6.enter_context(nc.sbuf_tensor("prev", [128, H * P], F32))
            prevb = ES6.enter_context(nc.sbuf_tensor("prevb", [128, H * P], BF16))
            sgm = ES6.enter_context(nc.sbuf_tensor("sgm", [128, H * P], BF16))
            htc2 = ES6.enter_context(nc.sbuf_tensor("htc2", [128, 2 * DI], BF16))
            Cc2 = ES6.enter_context(nc.sbuf_tensor("Cc2", [128, G * 128], BF16))
            ystg = ES6.enter_context(nc.sbuf_tensor("ystg", [128, 64 * 128], BF16))
            Lall = ES6.enter_context(nc.sbuf_tensor("Lall", [8, 128], F32))
            w8 = ES6.enter_context(nc.sbuf_tensor("w8", [8, 128], F32))
            wd8 = ES6.enter_context(nc.sbuf_tensor("wd8", [8, 1024], BF16))
            wb8 = ES6.enter_context(nc.sbuf_tensor("wb8", [128, 1024], F32))
            au4 = ES6.enter_context(nc.sbuf_tensor("au4", [128, 3 * 512], BF16))
            ebf = ES6.enter_context(nc.sbuf_tensor("ebf", [128, 3 * 512], BF16))
            eabf = ES6.enter_context(nc.sbuf_tensor("eabf", [128, 3 * 512], BF16))
            mbf = ES6.enter_context(nc.sbuf_tensor("mbf", [128, 3 * 512], BF16))
            mobf = ES6.enter_context(nc.sbuf_tensor("mobf", [128, 3 * 512], BF16))
            hdt2 = ES6.enter_context(nc.sbuf_tensor("hdt2", [128, 3 * 256], BF16))
            psE0 = ES6.enter_context(nc.psum_tensor("psE0", [128, 512], F32))
            psE1 = ES6.enter_context(nc.psum_tensor("psE1", [128, 512], F32))
            psEA0 = ES6.enter_context(nc.psum_tensor("psEA0", [128, 512], F32))
            psEA1 = ES6.enter_context(nc.psum_tensor("psEA1", [128, 512], F32))
            psY0 = ES6.enter_context(nc.psum_tensor("psY0", [128, 256], F32))
            psY1 = ES6.enter_context(nc.psum_tensor("psY1", [128, 256], F32))
            psW = ES6.enter_context(nc.psum_tensor("psW", [8, 128], F32))
            psWB = ES6.enter_context(nc.psum_tensor("psWB", [128, 512], F32))
            blk = ES6.enter_context(nc.Block("ph2b"))
            psY = [psY0, psY1]
            psE = [psE0, psE1]
            psEA = [psEA0, psEA1]

            def _(sy):
                tk.wait(sy, "ccL")
                tk.inc(sy.dma_start(Lall[:, :], ccLo.rearrange("(m h) o -> m (h o)", h=128)),
                       "lall", "lall", 16)
            blk.sync(_)

            def _(t):
                tk.wait(t, "lall")
                tk.inc(t.matmul(psW[:, :], t_selt[:, :], Lall[:, :], start=True, stop=True),
                       "t", "wmm")
            blk.tensor(_)

            def _(s):
                tk.wait(s, "wmm")
                tk.inc(s.activation(w8[:, :], psW[:, :], AF.Exp, bias=t_maskv[:, :]), "s", "wexp")
            blk.scalar(_)

            def _(v):
                tk.wait(v, "wexp")
                tk.inc(v.tensor_tensor(wd8[:, :].rearrange("p (m h) -> p m h", h=128),
                                       t_dm8[:, :].rearrange("p (m h) -> p m h", h=128),
                                       w8[:, :].unsqueeze(1).broadcast_to([8, 8, 128]), ALU.mult),
                       "v", "wd")
            blk.vector(_)

            def _(t):
                tk.wait(t, "wd")
                tk.inc(t.matmul(psWB[:, :], t_ao8[:, :], wd8[:, 0:512],
                                start=True, stop=True), "t", "wbmm0")
            blk.tensor(_)

            def _(v):
                tk.wait(v, "wbmm0")
                tk.inc(v.tensor_copy(wb8[:, 0:512], psWB[:, :]), "v", "wbev0")
            blk.vector(_)

            def _(t):
                tk.wait(t, "wbev0")
                tk.inc(t.matmul(psWB[:, :], t_ao8[:, :], wd8[:, 512:1024],
                                start=True, stop=True), "t", "wbmm1")
            blk.tensor(_)

            def _(v):
                tk.wait(v, "wbmm1")
                tk.inc(v.tensor_copy(wb8[:, 512:1024], psWB[:, :]), "v", "wb")
                v.memset(prev[:, :], 0.0)
                tk.inc(v.engine_nop(), "v", "previnit")
            blk.vector(_)

            for m in range(8):
                def _(sy, m=m):
                    tk.wait(sy, "ccS")
                    if m >= 1:
                        tk.wait(sy, f"sinadd_{m - 1}")
                    tk.inc(sy.dma_start(sgm[:, :], ccSo[128 * m:128 * (m + 1), :]),
                           "sgm", f"sgm_{m}", 16)
                blk.sync(_)

                def _(v, m=m):
                    tk.wait(v, f"sgm_{m}", "wb", "previnit")
                    v.tensor_tensor(sgm[:, :].rearrange("p (h q) -> p h q", q=64),
                                    sgm[:, :].rearrange("p (h q) -> p h q", q=64),
                                    wb8[:, 128 * m:128 * (m + 1)].unsqueeze(2).broadcast_to([128, 128, 64]),
                                    ALU.mult)
                    tk.inc(v.tensor_add(prev[:, :], prev[:, :], sgm[:, :]), "v", f"sinadd_{m}")
                blk.vector(_)

            if DEBUG:
                def _(g):
                    tk.wait(g, "sinadd_7")
                    tk.inc(g.dma_start(dbg["d_sin"], prev[:, :]), "g", "dbg7", 16)
                blk.gpsimd(_)

            def emit_au(v, c, s4):
                k = c * 32 + s4
                h0 = 4 * s4
                pr3 = k % 3
                tk.wait(v, f"htc2_{c}")
                if k >= 3:
                    tk.wait(v, f"2bmm_{(k - 3) // 32}_{(k - 3) % 32}")
                v.tensor_tensor(au4[:, pr3 * 512:(pr3 + 1) * 512].rearrange("p (a f) -> p a f", f=128),
                                t_U4[:, :].rearrange("p (a f) -> p a f", f=128),
                                t_afT[:, 128 * c + h0:128 * c + h0 + 4].unsqueeze(2).broadcast_to([128, 4, 128]),
                                ALU.mult)
                op = v.tensor_tensor(hdt2[:, pr3 * 256:(pr3 + 1) * 256].rearrange("p (a f) -> p a f", f=64),
                                     htc2[:, (c % 2) * DI + 64 * h0:(c % 2) * DI + 64 * h0 + 256]
                                     .rearrange("p (a f) -> p a f", f=64),
                                     t_dtT[:, 128 * c + h0:128 * c + h0 + 4].unsqueeze(2).broadcast_to([128, 4, 64]),
                                     ALU.mult)
                tk.bind(op, f"2bau_{c}_{s4}")

            def emit_emm(t, c, s4):
                k = c * 32 + s4
                par = s4 % 2
                pr3 = k % 3
                tk.wait(t, f"2bau_{c}_{s4}")
                if k >= 2:
                    tk.wait(t, f"2bexp_{(k - 2) // 32}_{(k - 2) % 32}")
                t.matmul(psE[par][:, :], t_Us[:, :], au4[:, pr3 * 512:(pr3 + 1) * 512],
                         start=True, stop=True)
                tk.bind(t.matmul(psEA[par][:, :], t_ao[:, :], au4[:, pr3 * 512:(pr3 + 1) * 512],
                                 start=True, stop=True), f"2bemm_{c}_{s4}")

            def emit_exp(s, c, s4):
                k = c * 32 + s4
                par = s4 % 2
                pr3 = k % 3
                tk.wait(s, f"2bemm_{c}_{s4}")
                if k >= 3:
                    tk.wait(s, f"2bmo_{(k - 3) // 32}_{(k - 3) % 32}")
                s.activation(ebf[:, pr3 * 512:(pr3 + 1) * 512], psE[par][:, :], AF.Exp)
                tk.bind(s.activation(eabf[:, pr3 * 512:(pr3 + 1) * 512], psEA[par][:, :], AF.Exp),
                        f"2bexp_{c}_{s4}")

            def emit_mo(v, c, s4):
                k = c * 32 + s4
                gg = (4 * s4) // 16
                pr3 = k % 3
                tk.wait(v, f"2bexp_{c}_{s4}", f"Cc2_{c}")
                v.tensor_tensor(mbf[:, pr3 * 512:(pr3 + 1) * 512].rearrange("p (a f) -> p a f", f=128),
                                ebf[:, pr3 * 512:(pr3 + 1) * 512].rearrange("p (a f) -> p a f", f=128),
                                t_STm[:, (gg * 8 + c) * 128:(gg * 8 + c + 1) * 128]
                                .unsqueeze(1).broadcast_to([128, 4, 128]), ALU.mult)
                tk.bind(v.tensor_tensor(mobf[:, pr3 * 512:(pr3 + 1) * 512].rearrange("p (a f) -> p a f", f=128),
                                        eabf[:, pr3 * 512:(pr3 + 1) * 512].rearrange("p (a f) -> p a f", f=128),
                                        Cc2[:, 128 * gg:128 * (gg + 1)].unsqueeze(1).broadcast_to([128, 4, 128]),
                                        ALU.mult), f"2bmo_{c}_{s4}")

            def emit_y(t, k):
                if k < 0:
                    return
                c, s4 = k // 32, k % 32
                h0 = 4 * s4
                par = s4 % 2
                pr3 = k % 3
                tk.wait(t, f"2bmo_{c}_{s4}", f"prevb_{c}")
                if k >= 2:
                    tk.wait(t, f"yev_{(k - 2) // 32}_{(k - 2) % 32}")
                for hp in range(4):
                    h = h0 + hp
                    r0 = 64 * (hp % 2)
                    c0 = 128 * (hp // 2)
                    t.matmul(psY[par][r0:r0 + 64, c0:c0 + 128],
                             hdt2[:, pr3 * 256 + 64 * hp:pr3 * 256 + 64 * (hp + 1)],
                             mbf[:, pr3 * 512 + 128 * hp:pr3 * 512 + 128 * (hp + 1)],
                             start=True, stop=False)
                    mm = t.matmul(psY[par][r0:r0 + 64, c0:c0 + 128],
                                  prevb[:, 64 * h:64 * (h + 1)],
                                  mobf[:, pr3 * 512 + 128 * hp:pr3 * 512 + 128 * (hp + 1)],
                                  start=False, stop=True)
                tk.bind(mm, f"2bmm_{c}_{s4}")

            def emit_yev(v, k):
                if k < 0:
                    return
                c, s4 = k // 32, k % 32
                par = s4 % 2
                tk.wait(v, f"2bmm_{c}_{s4}")
                if c >= 1 and s4 == 0:
                    tk.wait(v, f"yst_{c - 1}")
                tk.bind(v.tensor_copy(ystg[:, 256 * s4:256 * (s4 + 1)], psY[par][:, :]),
                        f"yev_{c}_{s4}")

            for c in range(NCH):
                def _(v, c=c):
                    tk.wait(v, "sinadd_7")
                    if c >= 1:
                        tk.wait(v, f"2bmm_{c - 1}_{31}", f"pupd_{c - 1}")
                    tk.inc(v.tensor_copy(prevb[:, :], prev[:, :]), "v", f"prevb_{c}")
                blk.vector(_)

                def _(sy, c=c):
                    if c == 0:
                        tk.wait(sy, "p1st")
                    if c >= 2:
                        tk.wait(sy, f"2bau_{c - 2}_{31}")
                    tk.inc(sy.dma_start(htc2[:, (c % 2) * DI:(c % 2 + 1) * DI], hidt_d[c]),
                           "htc2", f"htc2_{c}", 16)
                    if c >= 1:
                        tk.wait(sy, f"2bmo_{c - 1}_{31}")
                    tk.inc(sy.dma_start(Cc2[:, :].rearrange("p (g f) -> p g f", f=128),
                                        C_d.rearrange("(g p) t -> p g t", p=128)[:, :, 128 * c:128 * (c + 1)]),
                           "Cc2", f"Cc2_{c}", 16)
                    tk.wait(sy, "sinadd_7")
                    if c == 0:
                        tk.wait(sy, f"sst_{NCH - 1}")
                    if c >= 1:
                        tk.wait(sy, f"pupd_{c - 1}")
                    tk.inc(sy.dma_start(sgm[:, :], s_d[c]), "scld", f"scld_{c}", 16)
                blk.sync(_)

                for i in range(35):
                    if i < 32:
                        tk.reserve("v", f"2bau_{c}_{i}")
                    if 1 <= i <= 32:
                        tk.reserve("v", f"2bmo_{c}_{i - 1}")
                    if 3 <= i <= 34:
                        tk.reserve("v", f"yev_{c}_{i - 3}")
                for i in range(34):
                    if i < 32:
                        tk.reserve("t", f"2bemm_{c}_{i}")
                    if 2 <= i <= 33:
                        tk.reserve("t", f"2bmm_{c}_{i - 2}")
                for i in range(32):
                    tk.reserve("s", f"2bexp_{c}_{i}")

                def _(v, c=c):
                    for i in range(35):
                        if i < 32:
                            emit_au(v, c, i)
                        if 1 <= i <= 32:
                            emit_mo(v, c, i - 1)
                        if 3 <= i <= 34:
                            emit_yev(v, c * 32 + (i - 3))
                blk.vector(_)

                def _(t, c=c):
                    for i in range(34):
                        if i < 32:
                            emit_emm(t, c, i)
                        if 2 <= i <= 33:
                            emit_y(t, c * 32 + (i - 2))
                blk.tensor(_)

                def _(s, c=c):
                    for i in range(32):
                        emit_exp(s, c, i)
                blk.scalar(_)

                def _(v, c=c):
                    tk.wait(v, f"scld_{c}", f"2bmm_{c}_{31}")
                    dv = t_dcb[:, 128 * c:128 * (c + 1)].unsqueeze(2).broadcast_to([128, 128, 64])
                    v.tensor_tensor(prev[:, :].rearrange("p (h q) -> p h q", q=64),
                                    prev[:, :].rearrange("p (h q) -> p h q", q=64), dv, ALU.mult)
                    tk.inc(v.tensor_add(prev[:, :], prev[:, :], sgm[:, :]), "v", f"pupd_{c}")
                blk.vector(_)

                def _(g, c=c):
                    tk.wait(g, f"yev_{c}_{31}")
                    dstY = y_d.rearrange("(o p) t -> p o t", p=128)[:, :, 128 * c:128 * (c + 1)]
                    tk.inc(g.dma_start(dstY, ystg[:, :].rearrange("p (o f) -> p o f", f=128)),
                           "yst", f"yst_{c}", 16)
                blk.gpsimd(_)

    if STOP == "p2b":
        XS.close()
        return nc
    with nc.sbuf_tensor("y2", [128, 64 * TOK], BF16) as y2:
        ES7 = contextlib.ExitStack()
        with ES7:
            ygl = ES7.enter_context(nc.sbuf_tensor("ygl", [128, 8 * TOK], BF16))
            hld = ES7.enter_context(nc.sbuf_tensor("hld", [128, 2 * TOK], BF16))
            ggl = ES7.enter_context(nc.sbuf_tensor("ggl", [128, 2 * TOK], BF16))
            sqb = ES7.enter_context(nc.sbuf_tensor("sqb", [128, 2 * TOK], BF16))
            rsb = ES7.enter_context(nc.sbuf_tensor("rsb", [128, TOK], F32))
            rs0 = ES7.enter_context(nc.sbuf_tensor("rs0", [128, TOK], F32))
            rtmp = ES7.enter_context(nc.sbuf_tensor("rtmp", [128, TOK], F32))
            pg0 = ES7.enter_context(nc.psum_tensor("pg0", [128, 512], F32))
            pg1 = ES7.enter_context(nc.psum_tensor("pg1", [128, 512], F32))
            blk = ES7.enter_context(nc.Block("ph3"))
            for gg in range(8):
                for j in range(8):
                    oc = gg * 8 + j

                    def _(sy, gg=gg, j=j, oc=oc):
                        if oc == 0:
                            tk.wait(sy, f"yst_{NCH - 1}")
                        if gg >= 1 and j == 0:
                            tk.wait(sy, f"y2_{gg - 1}")
                        if oc >= 2:
                            tk.wait(sy, f"yg_{oc - 2}")
                        tk.inc(sy.dma_start(ygl[:, j * TOK:(j + 1) * TOK],
                                            y_d[128 * oc:128 * (oc + 1), :]), "p3y", f"p3y_{oc}", 16)
                        tk.inc(sy.dma_start(ggl[:, (oc % 2) * TOK:(oc % 2 + 1) * TOK],
                                            gate_d[128 * oc:128 * (oc + 1), :]), "p3g", f"p3g_{oc}", 16)
                        tk.inc(sy.dma_start(hld[:, (oc % 2) * TOK:(oc % 2 + 1) * TOK],
                                            hidf_d[128 * oc:128 * (oc + 1), :]), "p3h", f"p3h_{oc}", 16)
                    blk.sync(_)

                    def _(v, gg=gg, j=j, oc=oc):
                        tk.wait(v, f"p3y_{oc}", f"p3g_{oc}", f"p3h_{oc}")
                        v.scalar_tensor_tensor(ygl[:, j * TOK:(j + 1) * TOK],
                                               hld[:, (oc % 2) * TOK:(oc % 2 + 1) * TOK],
                                               t_dpoc[:, oc:oc + 1],
                                               ygl[:, j * TOK:(j + 1) * TOK], ALU.mult, ALU.add)
                        tk.inc(v.tensor_mul(ygl[:, j * TOK:(j + 1) * TOK],
                                            ygl[:, j * TOK:(j + 1) * TOK],
                                            ggl[:, (oc % 2) * TOK:(oc % 2 + 1) * TOK]),
                               "v", f"yg_{oc}")
                    blk.vector(_)

                    def _(s, gg=gg, j=j, oc=oc):
                        tk.wait(s, f"yg_{oc}")
                        if oc >= 2:
                            tk.wait(s, f"p3mm_{oc - 2}")
                        tk.inc(s.activation(sqb[:, (oc % 2) * TOK:(oc % 2 + 1) * TOK],
                                            ygl[:, j * TOK:(j + 1) * TOK], AF.Square),
                               "s", f"p3sq_{oc}")
                    blk.scalar(_)

                    def _(t, gg=gg, j=j, oc=oc):
                        tk.wait(t, f"p3sq_{oc}")
                        if j == 0 and gg >= 1:
                            tk.wait(t, f"rsarg_{gg - 1}")
                        t.matmul(pg0[:, :], t_ao[:, :], sqb[:, (oc % 2) * TOK:(oc % 2) * TOK + 512],
                                 start=(j == 0), stop=(j == 7))
                        mm = t.matmul(pg1[:, :], t_ao[:, :],
                                      sqb[:, (oc % 2) * TOK + 512:(oc % 2 + 1) * TOK],
                                      start=(j == 0), stop=(j == 7))
                        tk.inc(mm, "t", f"p3mm_{oc}")
                    blk.tensor(_)

                def _(v, gg=gg):
                    tk.wait(v, f"p3mm_{gg * 8 + 7}")
                    v.tensor_scalar(rsb[:, 0:512], pg0[:, :], 1.0 / 1024, EPS, ALU.mult, ALU.add)
                    tk.inc(v.tensor_scalar(rsb[:, 512:1024], pg1[:, :], 1.0 / 1024, EPS,
                                           ALU.mult, ALU.add), "v", f"rsarg_{gg}")
                blk.vector(_)

                def _(s, gg=gg):
                    tk.wait(s, f"rsarg_{gg}")
                    tk.inc(s.activation(rs0[:, :], rsb[:, :], AF.Sqrt),
                           "s", f"rsg_{gg}")
                blk.scalar(_)

                def _(v, gg=gg):
                    tk.wait(v, f"rsg_{gg}")
                    v.reciprocal(rs0[:, :], rs0[:, :])
                    v.tensor_mul(rtmp[:, :], rs0[:, :], rs0[:, :])
                    v.tensor_mul(rtmp[:, :], rtmp[:, :], rsb[:, :])
                    v.tensor_scalar(rtmp[:, :], rtmp[:, :], -0.5, 1.5, ALU.mult, ALU.add)
                    v.tensor_mul(rs0[:, :], rs0[:, :], rtmp[:, :])
                    for j in range(8):
                        oc = gg * 8 + j
                        v.tensor_mul(ygl[:, j * TOK:(j + 1) * TOK], ygl[:, j * TOK:(j + 1) * TOK],
                                     rs0[:, :])
                        v.tensor_scalar(y2[:, oc * TOK:(oc + 1) * TOK], ygl[:, j * TOK:(j + 1) * TOK],
                                        t_gnw[:, oc:oc + 1], None, ALU.mult)
                    tk.inc(v.engine_nop(), "v", f"y2_{gg}")
                blk.vector(_)

            if DEBUG:
                def _(g):
                    tk.wait(g, "y2_7")
                    yv = dbg["d_y2"].rearrange("(c p) t -> c p t", p=128)
                    for oc in range(64):
                        tk.inc(g.dma_start(yv[oc], y2[:, oc * TOK:(oc + 1) * TOK]), "g", "dbg8", 16)
                blk.gpsimd(_)

        ES8 = contextlib.ExitStack()
        with ES8:
            ow = ES8.enter_context(nc.sbuf_tensor("ow", [128, 2 * 64 * 128], BF16))
            x4 = ES8.enter_context(nc.sbuf_tensor("x4", [128, 2 * 512], F32))
            gg4 = ES8.enter_context(nc.sbuf_tensor("gg4", [128, 2 * 512], BF16))
            ot4 = ES8.enter_context(nc.sbuf_tensor("ot4", [128, 2 * 512], F32))
            sc4 = ES8.enter_context(nc.sbuf_tensor("sc4", [128, 512], F32))
            po0 = ES8.enter_context(nc.psum_tensor("po0", [128, 512], F32))
            po1 = ES8.enter_context(nc.psum_tensor("po1", [128, 512], F32))
            blk = ES8.enter_context(nc.Block("ph4"))
            po = [po0, po1]
            for oc in range(32):
                def _(g, oc=oc):
                    if oc >= 2:
                        tk.wait(g, f"4mm_{oc - 2}_1")
                    dst = ow[:, (oc % 2) * 64 * 128:(oc % 2 + 1) * 64 * 128]
                    tk.inc(g.dma_start(dst, WoutT[oc]), f"ow{oc % 2}", f"ow_{oc}", 16)
                blk.gpsimd(_)

                for tt in range(2):
                    kb = 2 * oc + tt

                    def _(sy, oc=oc, tt=tt, kb=kb):
                        if kb == 0:
                            tk.wait(sy, f"yst_{NCH - 1}")
                        if kb >= 2:
                            tk.wait(sy, f"4ev_{(kb - 2) // 2}_{(kb - 2) % 2}")
                        tk.inc(sy.dma_start(x4[:, (kb % 2) * 512:(kb % 2 + 1) * 512],
                                            xfm[128 * oc:128 * (oc + 1), 3 + 512 * tt:3 + 512 * (tt + 1)]),
                               f"x4{kb % 2}", f"4x_{oc}_{tt}", 16)
                        tk.inc(sy.dma_start(gg4[:, (kb % 2) * 512:(kb % 2 + 1) * 512],
                                            g_d[128 * oc:128 * (oc + 1), 512 * tt:512 * (tt + 1)]),
                               f"g4{kb % 2}", f"4g_{oc}_{tt}", 16)
                    blk.sync(_)

                    def _(t, oc=oc, tt=tt, kb=kb):
                        tk.wait(t, f"ow_{oc}")
                        if kb >= 2:
                            tk.wait(t, f"4ev_{(kb - 2) // 2}_{(kb - 2) % 2}")
                        base = (oc % 2) * 64 * 128
                        for ch in range(64):
                            mm = t.matmul(po[tt][:, :], ow[:, base + 128 * ch:base + 128 * (ch + 1)],
                                          y2[:, ch * TOK + 512 * tt:ch * TOK + 512 * (tt + 1)],
                                          start=(ch == 0), stop=(ch == 63))
                        tk.inc(mm, "t", f"4mm_{oc}_{tt}")
                    blk.tensor(_)

                    def _(v, oc=oc, tt=tt, kb=kb):
                        tk.wait(v, f"4mm_{oc}_{tt}", f"4x_{oc}_{tt}", f"4g_{oc}_{tt}")
                        if kb >= 2:
                            tk.wait(v, f"4st_{(kb - 2) // 2}_{(kb - 2) % 2}")
                        xt = x4[:, (kb % 2) * 512:(kb % 2 + 1) * 512]
                        v.tensor_sub(sc4[:, :], xt, t_mu[:, 3 + 512 * tt:3 + 512 * (tt + 1)])
                        v.tensor_mul(sc4[:, :], sc4[:, :], t_c2[:, 3 + 512 * tt:3 + 512 * (tt + 1)])
                        v.tensor_add(sc4[:, :], sc4[:, :], po[tt][:, :])
                        v.tensor_mul(sc4[:, :], sc4[:, :], gg4[:, (kb % 2) * 512:(kb % 2 + 1) * 512])
                        tk.inc(v.tensor_add(ot4[:, (kb % 2) * 512:(kb % 2 + 1) * 512], sc4[:, :], xt),
                               "v", f"4ev_{oc}_{tt}")
                    blk.vector(_)

                    def _(s, oc=oc, tt=tt, kb=kb):
                        tk.wait(s, f"4ev_{oc}_{tt}")
                        tk.inc(s.dma_start(out_fm[128 * oc:128 * (oc + 1), 512 * tt:512 * (tt + 1)],
                                           ot4[:, (kb % 2) * 512:(kb % 2 + 1) * 512]),
                               f"ost{kb % 2}", f"4st_{oc}_{tt}", 16)
                    blk.scalar(_)

            def _(s):
                tk.wait(s, "4st_31_0", "4st_31_1")
            blk.scalar(_)

        if DEBUG:
            with nc.Block("phdbg") as blk:
                def _(g):
                    tk.inc(g.dma_start(dbg["d_yd"], y_d), "g", "dbgf1", 16)
                    tk.inc(g.dma_start(dbg["d_gd"], g_d), "g", "dbgf2", 16)
                    tk.inc(g.dma_start(dbg["d_gated"], gate_d), "g", "dbgf3", 16)
                    yv = dbg["d_y2b"].rearrange("(c p) t -> c p t", p=128)
                    for oc in range(64):
                        tk.inc(g.dma_start(yv[oc], y2[:, oc * TOK:(oc + 1) * TOK]), "g", "dbgf4", 16)
                    tk.wait(g, "dbgf4")
                blk.gpsimd(_)

    return nc


_NC_CACHE = {}


def _get_nc():
    if "nc" not in _NC_CACHE:
        _NC_CACHE["nc"] = build_nc()
    return _NC_CACHE["nc"]


def kernel(**inputs):
    x = np.ascontiguousarray(np.asarray(inputs["L0_attn_x"], dtype=np.float32))
    Win = np.asarray(inputs["L0_attn_in_proj_w"], dtype=np.float32)
    Wout = np.asarray(inputs["L0_attn_out_proj_w"], dtype=np.float32)
    Wff = np.ascontiguousarray(np.asarray(inputs["L0_ffn_kernel"], dtype=np.float32))
    cw = np.asarray(inputs["L0_attn_conv_w"], dtype=np.float32)
    cb = np.asarray(inputs["L0_attn_conv_b"], dtype=np.float32)
    nw = np.asarray(inputs["L0_attn_norm_w"], dtype=np.float32)
    dtb = np.asarray(inputs["L0_attn_dt_bias"], dtype=np.float32)
    alog = np.asarray(inputs["L0_attn_A_log"], dtype=np.float32)
    dp = np.asarray(inputs["L0_attn_Dp"], dtype=np.float32)
    gnw = np.asarray(inputs["L0_attn_gnorm_w"], dtype=np.float32)
    gate = np.asarray(inputs["L0_ffn_gate"], dtype=np.float32)
    ffb = np.asarray(inputs["L0_ffn_bias"], dtype=np.float32)
    lA = np.asarray(inputs["L0_ffn_lora_A"], dtype=np.float32)
    lB = np.asarray(inputs["L0_ffn_lora_B"], dtype=np.float32)
    lAs = np.asarray(inputs["L0_ffn_lora_A_shared"], dtype=np.float32)
    lBs = np.asarray(inputs["L0_ffn_lora_B_shared"], dtype=np.float32)

    WinT = np.ascontiguousarray(
        Win.T.reshape(32, 128, PROJD // 128, 128).transpose(2, 1, 0, 3).reshape(PROJD // 128, 128, DM))
    WoutT = np.ascontiguousarray(
        Wout.T.reshape(64, 128, 32, 128).transpose(2, 1, 0, 3).reshape(32, 128, DI))
    Wff = np.ascontiguousarray(
        Wff.reshape(32, 128, 32, 128).transpose(2, 1, 0, 3).reshape(32, 128, DM))
    nw_re = np.ascontiguousarray(nw.reshape(NCD, 128).T)
    cw_re = np.ascontiguousarray(cw.reshape(80, 128, 4).transpose(1, 0, 2).reshape(128, 320))
    cb_re = np.ascontiguousarray(cb.reshape(80, 128).T)
    dtb_re = np.ascontiguousarray(dtb.reshape(128, 1))
    alog_re = np.ascontiguousarray(alog.reshape(128, 1))
    dp_re = np.ascontiguousarray(dp.reshape(1, 128))
    gnw_re = np.ascontiguousarray(gnw.reshape(64, 128).T)
    dpoc_re = np.ascontiguousarray(np.repeat(dp.reshape(64, 2), 64, axis=1).reshape(64, 128).T)
    ffnb_re = np.ascontiguousarray(ffb.reshape(NCD, 128).T)
    lA_re = np.ascontiguousarray(
        lA.transpose(0, 2, 1).reshape(16, NCD, 128, 32).transpose(0, 2, 1, 3).reshape(16, 128, NCD * 32))
    lB_re = np.ascontiguousarray(
        lB.reshape(16, 4, 1024, 32).transpose(0, 1, 3, 2).reshape(16, 128, 1024))
    lAs_re = np.ascontiguousarray(
        lAs.T.reshape(NCD, 128, 32).transpose(1, 0, 2).reshape(128, NCD * 32))
    lBs_re = np.ascontiguousarray(
        lBs.reshape(4, 1024, 32).transpose(0, 2, 1).reshape(128, 1024))

    in_maps = []
    for k in range(8):
        b, q = k // 4, k % 4
        xT = np.ascontiguousarray(x[b].T)
        halo = np.zeros((DM, 3), np.float32) if q == 0 else xT[:, 1024 * q - 3:1024 * q]
        xfm_k = np.ascontiguousarray(np.concatenate([halo, xT[:, 1024 * q:1024 * (q + 1)]], axis=1))
        selt = np.zeros((8, 8), np.float32)
        maskv = np.full((8, 1), -1e9, np.float32)
        for m in range(8):
            if m // 4 == b and m < k:
                maskv[m, 0] = 0.0
            for j in range(8):
                if j // 4 == b and m < j < k:
                    selt[j, m] = 1.0
        in_maps.append({
            "xfm": xfm_k, "WinT": WinT, "WoutT": WoutT, "Wffn": Wff,
            "nw_re": nw_re, "cw_re": cw_re, "cb_re": cb_re, "dtb_re": dtb_re,
            "alog_re": alog_re, "dp_re": dp_re, "gnw_re": gnw_re, "dpoc_re": dpoc_re,
            "g16_re": np.ascontiguousarray(gate[k].reshape(1, 16)),
            "selt": selt, "maskv": maskv,
            "lA_re": lA_re, "lB_re": lB_re, "lAs_re": lAs_re, "lBs_re": lBs_re,
            "ffnb_re": ffnb_re,
        })

    nc = _get_nc()
    res = run_bass_kernel_spmd(nc, in_maps, core_ids=list(range(8)))
    out = np.empty((2, 4096, 4096), dtype=np.float32)
    for k in range(8):
        b, q = k // 4, k % 4
        out[b, 1024 * q:1024 * (q + 1), :] = res.results[k]["out_fm"].T
    if DEBUG:
        kernel.last_results = res
    return out


if __name__ == "__main__":
    nc = build_nc()
    print("instructions:", nc.n_instructions())
